# version 17
# baseline (speedup 1.0000x reference)
import sys
from contextlib import ExitStack

sys.path.insert(0, "/opt/trn_rl_repo")

import numpy as np
import ml_dtypes
import concourse.bass as bass
import concourse.mybir as mybir
from concourse.bass_utils import run_bass_kernel_spmd

F32 = mybir.dt.float32
BF16 = mybir.dt.bfloat16
AF = mybir.ActivationFunctionType
ALU = mybir.AluOpType
AXX = mybir.AxisListType.X

NCORE = 8
D = 2048
T = 2048
NCH = 16
G = 512
NG = 4
H = 16
DK = 128
HF = 8192
AH = 64
KD = 16
EPS = 1e-5
LOG_SQRT_DK = 0.5 * float(np.log(DK))
RT2 = float(np.sqrt(2.0))
AGW = H * DK + H


def _hadamard(nn):
    i = np.arange(nn)
    pc = np.zeros((nn, nn), dtype=np.int64)
    a = i[:, None] & i[None, :]
    while a.any():
        pc += a & 1
        a >>= 1
    return np.where(pc % 2 == 0, 1.0, -1.0).astype(np.float32)


def _host_consts(core_id):
    bf = ml_dtypes.bfloat16
    h128s = (_hadamard(128) * (2.0 ** -7)).astype(bf)
    h64i2 = (np.kron(_hadamard(64), np.eye(2, dtype=np.float32))
             * np.sqrt(2.0)).astype(bf)
    ident = np.eye(128, dtype=np.float32).astype(bf)
    maskut = (np.arange(128)[:, None] <= np.arange(128)[None, :]).astype(np.float32)
    cm = np.zeros((128, 8), dtype=np.float32)
    cm[:, :core_id] = 1.0
    return {
        "c_h128s": h128s, "c_h64i2": h64i2, "c_ident": ident,
        "c_maskut": maskut, "c_cmask": cm,
        "c_icmask": (1.0 - cm).astype(np.float32),
    }


def _build(dump=False):
    nc = bass.Bass("TRN2", target_bir_lowering=False, debug=False)

    ap_x = nc.dram_tensor("x", [T, D], F32, kind="ExternalInput").ap()
    ap_w = {nm: nc.dram_tensor(nm, [D, D], F32, kind="ExternalInput").ap()
            for nm in ("wq", "wk", "wv", "wg", "wo")}
    ap_w1 = nc.dram_tensor("w1", [HF, D], F32, kind="ExternalInput").ap()
    ap_w2 = nc.dram_tensor("w2", [D, HF], F32, kind="ExternalInput").ap()
    ap_w3 = nc.dram_tensor("w3", [HF, D], F32, kind="ExternalInput").ap()
    c_h128s = nc.dram_tensor("c_h128s", [128, 128], BF16, kind="ExternalInput").ap()
    c_h64i2 = nc.dram_tensor("c_h64i2", [128, 128], BF16, kind="ExternalInput").ap()
    c_ident = nc.dram_tensor("c_ident", [128, 128], BF16, kind="ExternalInput").ap()
    c_maskut = nc.dram_tensor("c_maskut", [128, 128], F32, kind="ExternalInput").ap()
    c_cmask = nc.dram_tensor("c_cmask", [128, 8], F32, kind="ExternalInput").ap()
    c_icmask = nc.dram_tensor("c_icmask", [128, 8], F32, kind="ExternalInput").ap()
    ap_out = nc.dram_tensor("out", [T, D], F32, kind="ExternalOutput").ap()

    ks = "ExternalOutput" if dump else "Internal"
    sp_q = nc.dram_tensor("sp_q", [KD, 128, T], BF16, kind=ks).ap()
    sp_k = nc.dram_tensor("sp_k", [KD, 128, T], BF16, kind=ks).ap()
    sp_g = nc.dram_tensor("sp_g", [KD, 128, T], BF16, kind=ks).ap()
    sp_v = nc.dram_tensor("sp_v", [NCH, 128, D], BF16, kind=ks).ap()
    sp_qd = nc.dram_tensor("sp_qd", [KD, 128, T], BF16, kind=ks).ap()
    sp_o = nc.dram_tensor("sp_o", [KD, 128, T], F32, kind=ks).ap()
    sp_r2 = nc.dram_tensor("sp_r2", [NCH, 128, D], F32, kind=ks).ap()
    sp_h = nc.dram_tensor("sp_h", [AH, 128, T], BF16, kind=ks).ap()
    sp_w2t = nc.dram_tensor("sp_w2t", [AH, 128, D], BF16, kind=ks).ap()
    sp_sdbg = nc.dram_tensor("sp_sdbg", [NCH, 128, H * DK], F32, kind=ks).ap()
    sp_kdbg = nc.dram_tensor("sp_kdbg", [2, 128, H * DK // 2], F32, kind=ks).ap()
    ag_in = nc.dram_tensor("ag_in", [128, AGW], F32, kind="Internal")
    ag_out = nc.dram_tensor("ag_out", [NCORE * 128, AGW], F32, kind="Internal",
                            addr_space="Shared")

    xr = ap_x.rearrange("(c p) d -> c p d", p=128)
    outr = ap_out.rearrange("(c p) d -> c p d", p=128)

    dma_sem_names = (
        ["c_ld", "x0", "x1", "ps0", "ps1", "rg0", "rg1", "rg2", "rv",
         "vs0", "vs1", "vs2", "vs3", "w2r", "w2s", "g2l", "fl", "agr", "agi",
         "wo", "oq", "xr", "r2s", "p4w0", "p4w1", "ys0", "ys1", "yl",
         "r50", "r51", "wt0", "wt1", "ob0", "ob1", "ob2", "ob3", "dbg"])
    n = {s: 0 for s in
         ["pe", "dv", "ac", "cc", "bar"] + dma_sem_names}

    with ExitStack() as top:
        sem = {s: top.enter_context(nc.semaphore("s_" + s)) for s in n}
        blk = top.enter_context(nc.Block())

        def I(instr, s, k=1):
            instr.then_inc(sem[s], k)
            n[s] += k
            return n[s]

        def W(eng, s, v):
            assert v <= n[s], f"wait {s}>={v} but only {n[s]} emitted"
            if v > 0:
                eng.wait_ge(sem[s], v)

        def FE(e, s, ins):
            v = I(ins, s, 1)
            e.wait_ge(sem[s], v)

        def dma(eng, q, out, in_):
            return I(eng.dma_start(out=out, in_=in_), q, 16)

        def barrier():
            fins = [(s, n[s]) for s in
                    ["pe", "dv", "ac", "cc"] + dma_sem_names if n[s] > 0]

            def _b(e):
                for s, v in fins:
                    W(e, s, v)
            blk.sync(_b)
            blk.gpsimd(_b)
            blk.tensor(_b)
            blk.vector(_b)
            blk.scalar(_b)

        idb = top.enter_context(nc.sbuf_tensor("idb", [128, 128], BF16))
        h128s = top.enter_context(nc.sbuf_tensor("h128s", [128, 128], BF16))
        h64i2 = top.enter_context(nc.sbuf_tensor("h64i2", [128, 128], BF16))
        maskut = top.enter_context(nc.sbuf_tensor("maskut", [128, 128], F32))
        cmask = top.enter_context(nc.sbuf_tensor("cmask", [128, 8], F32))
        icmask = top.enter_context(nc.sbuf_tensor("icmask", [128, 8], F32))
        zer = top.enter_context(nc.sbuf_tensor("zer", [128, 128], F32))
        sin_bf = top.enter_context(nc.sbuf_tensor("sin_bf", [128, H * DK], BF16))
        stc = top.enter_context(nc.sbuf_tensor("stc", [128, 64], F32))

        @blk.sync
        def _(e):
            for src, dst in ((c_ident, idb), (c_h128s, h128s), (c_h64i2, h64i2),
                             (c_maskut, maskut), (c_cmask, cmask),
                             (c_icmask, icmask)):
                dma(e, "c_ld", dst[:, :], src)

        cvals = [EPS, 1.0, -LOG_SQRT_DK]
        cbuf = top.enter_context(nc.sbuf_tensor("cbuf", [128, len(cvals)], F32))
        for ci_, v_ in enumerate(cvals):
            nc.const_aps.register(F32, v_, cbuf[:, ci_:ci_ + 1])

        @blk.vector
        def _(e):
            e.memset(zer[:, :], 0.0)
            ins = None
            for ci_, v_ in enumerate(cvals):
                ins = e.memset(cbuf[:, ci_:ci_ + 1], v_)
            I(ins, "dv", 1)

        pb = [top.enter_context(nc.psum_tensor(f"pb{i}", [128, 512], F32))
              for i in range(8)]

        with ExitStack() as s1:
            xnf = s1.enter_context(nc.sbuf_tensor("xnf", [128, KD * T], BF16))
            xbuf = [s1.enter_context(nc.sbuf_tensor(f"xb{i}", [128, D], F32))
                    for i in range(2)]
            xnb = [s1.enter_context(nc.sbuf_tensor(f"xnb{i}", [128, D], BF16))
                   for i in range(2)]
            scrA = s1.enter_context(nc.sbuf_tensor("scrA", [128, D], F32))
            scrB = s1.enter_context(nc.sbuf_tensor("scrB", [128, D], F32))

            xload = [0] * NCH
            var_done = [0] * NCH
            rstd_done = [0] * NCH
            xn_done = [0] * NCH
            tmm_done = [[0] * 4 for _ in range(NCH)]
            ev_done = [[0] * 4 for _ in range(NCH)]

            for t in range(NCH):
                @blk.sync
                def _(e, t=t):
                    if t >= 2:
                        W(e, "dv", xn_done[t - 2])
                    xload[t] = dma(e, "x%d" % (t % 2), xbuf[t % 2][:, :], xr[t])

                @blk.vector
                def _(e, t=t):
                    W(e, "x%d" % (t % 2), xload[t])
                    o = (t % 2) * 4
                    e.tensor_scalar(out=scrA[:, :], in0=xbuf[t % 2][:, :],
                                    scalar1=1.0, scalar2=0.0, op0=ALU.mult,
                                    op1=ALU.add, accum_out=stc[:, o:o + 1])
                    ins = e.scalar_tensor_tensor(
                        out=scrB[:, :], in0=xbuf[t % 2][:, :], scalar=1.0,
                        in1=xbuf[t % 2][:, :], op0=ALU.bypass, op1=ALU.mult,
                        accum_out=stc[:, o + 1:o + 2])
                    var_done[t] = I(ins, "dv", 1)

                @blk.scalar
                def _(e, t=t):
                    W(e, "dv", var_done[t])
                    o = (t % 2) * 4
                    FE(e, "ac", e.mul(stc[:, o:o + 1], stc[:, o:o + 1], 1.0 / D))
                    FE(e, "ac", e.activation(stc[:, o + 2:o + 3],
                                             stc[:, o:o + 1], AF.Square))
                    FE(e, "ac", e.mul(stc[:, o + 2:o + 3],
                                      stc[:, o + 2:o + 3], -1.0))
                    FE(e, "ac", e.activation(stc[:, o + 3:o + 4],
                                             stc[:, o + 1:o + 2], AF.Identity,
                                             bias=stc[:, o + 2:o + 3],
                                             scale=1.0 / D))
                    FE(e, "ac", e.activation(stc[:, o + 3:o + 4],
                                             stc[:, o + 3:o + 4], AF.Ln,
                                             bias=EPS))
                    ins = e.activation(stc[:, o + 3:o + 4],
                                       stc[:, o + 3:o + 4], AF.Exp, scale=-0.5)
                    rstd_done[t] = I(ins, "ac", 1)

                @blk.vector
                def _(e, t=t):
                    W(e, "ac", rstd_done[t])
                    if t >= 2:
                        W(e, "pe", tmm_done[t - 2][3])
                    o = (t % 2) * 4
                    ins = e.tensor_scalar(
                        out=xnb[t % 2][:, :], in0=xbuf[t % 2][:, :],
                        scalar1=stc[:, o:o + 1], scalar2=stc[:, o + 3:o + 4],
                        op0=ALU.subtract, op1=ALU.mult)
                    xn_done[t] = I(ins, "dv", 1)

                @blk.tensor
                def _(e, t=t):
                    if t == 0:
                        W(e, "c_ld", 96)
                    W(e, "dv", xn_done[t])
                    for grp in range(4):
                        if t >= 1:
                            W(e, "dv", ev_done[t - 1][grp])
                        ins = None
                        for d4 in range(4):
                            dj = grp * 4 + d4
                            ins = e.matmul(pb[grp][:, d4 * 128:(d4 + 1) * 128],
                                           xnb[t % 2][:, dj * 128:(dj + 1) * 128],
                                           idb[:, :], start=True, stop=True)
                        tmm_done[t][grp] = I(ins, "pe", 1)

                @blk.vector
                def _(e, t=t):
                    for grp in range(4):
                        W(e, "pe", tmm_done[t][grp])
                        dst = xnf[:, :].rearrange("p (j c) -> p j c", j=KD)[
                            :, grp * 4:(grp + 1) * 4, t * 128:(t + 1) * 128]
                        ins = e.tensor_copy(
                            out=dst,
                            in_=pb[grp][:, :].rearrange("p (j c) -> p j c", j=4))
                        ev_done[t][grp] = I(ins, "dv", 1)

            xnf_done = ev_done[NCH - 1][3]

            ring = s1.enter_context(nc.sbuf_tensor("ring", [128, 3 * KD * 128], BF16))
            pstag = [s1.enter_context(nc.sbuf_tensor(f"pst{i}", [128, D], BF16))
                     for i in range(2)]

            ring_guard = [0, 0, 0]
            stag_guard = [0, 0]

            def proj_fm(wap, sp, prev_bank_dv):
                wl = [0] * KD
                mmj = [0] * KD
                evj = [0] * KD
                spj = [0] * KD
                wr = wap.rearrange("(k p) d -> k p d", p=128)
                rv = ring[:, :].rearrange("p (s k c) -> p s k c", s=3, k=KD)
                for j in range(KD):
                    @blk.gpsimd
                    def _(e, j=j):
                        if ring_guard[j % 3]:
                            W(e, "pe", ring_guard[j % 3])
                        wl[j] = dma(e, "rg%d" % (j % 3), rv[:, j % 3],
                                    wr[:, :, j * 128:(j + 1) * 128]
                                    .transpose([1, 0, 2]))

                    @blk.tensor
                    def _(e, j=j):
                        W(e, "rg%d" % (j % 3), wl[j])
                        if j == 0:
                            W(e, "dv", prev_bank_dv)
                            W(e, "dv", xnf_done)
                        else:
                            W(e, "dv", evj[j - 1])
                        ins = None
                        for g in range(NG):
                            for k in range(KD):
                                ins = e.matmul(
                                    pb[g][:, :],
                                    ring[:, ((j % 3) * KD + k) * 128:
                                         ((j % 3) * KD + k) * 128 + 128],
                                    xnf[:, k * T + g * G:k * T + (g + 1) * G],
                                    start=(k == 0), stop=(k == KD - 1))
                        mmj[j] = I(ins, "pe", 1)
                    ring_guard[j % 3] = mmj[j]

                    @blk.vector
                    def _(e, j=j):
                        W(e, "pe", mmj[j])
                        if stag_guard[j % 2]:
                            W(e, "ps%d" % (j % 2), stag_guard[j % 2])
                        ins = None
                        for g in range(NG):
                            ins = e.tensor_copy(
                                out=pstag[j % 2][:, g * G:(g + 1) * G],
                                in_=pb[g][:, :])
                        evj[j] = I(ins, "dv", 1)

                    @blk.sync
                    def _(e, j=j):
                        W(e, "dv", evj[j])
                        spj[j] = dma(e, "ps%d" % (j % 2), sp[j],
                                     pstag[j % 2][:, :])
                    stag_guard[j % 2] = spj[j]
                return evj[KD - 1]

            prev_ev = xnf_done
            for wnm, sp in (("wq", sp_q), ("wk", sp_k), ("wg", sp_g)):
                prev_ev = proj_fm(ap_w[wnm], sp, prev_ev)

            ringv = s1.enter_context(nc.sbuf_tensor("ringv", [128, KD * G], BF16))
            vstag = [s1.enter_context(nc.sbuf_tensor(f"vst{i}", [128, G], BF16))
                     for i in range(4)]
            wvr = ap_w["wv"].rearrange("(k p) d -> k p d", p=128)
            rvv = ringv[:, :].rearrange("p (k c) -> p k c", k=KD)
            vwl = [0] * NG
            vmm = [[0] * NCH for _ in range(NG)]
            vev = [[0] * NCH for _ in range(NG)]
            vsp = [[0] * NCH for _ in range(NG)]
            for dvg in range(NG):
                @blk.gpsimd
                def _(e, dvg=dvg):
                    if dvg >= 1:
                        W(e, "pe", vmm[dvg - 1][NCH - 1])
                    vwl[dvg] = dma(e, "rv", rvv,
                                   wvr[:, :, dvg * G:(dvg + 1) * G]
                                   .transpose([1, 0, 2]))

                for tc in range(NCH):
                    @blk.tensor
                    def _(e, dvg=dvg, tc=tc):
                        if tc == 0:
                            W(e, "rv", vwl[dvg])
                        if dvg == 0 and tc < 4:
                            W(e, "dv", prev_ev)
                        else:
                            pv = vev[dvg][tc - 4] if tc >= 4 else \
                                vev[dvg - 1][NCH - 4 + tc]
                            W(e, "dv", pv)
                        ins = None
                        for k in range(KD):
                            ins = e.matmul(
                                pb[tc % 4][:, :],
                                xnf[:, k * T + tc * 128:k * T + (tc + 1) * 128],
                                ringv[:, k * G:(k + 1) * G],
                                start=(k == 0), stop=(k == KD - 1))
                        vmm[dvg][tc] = I(ins, "pe", 1)

                    @blk.vector
                    def _(e, dvg=dvg, tc=tc):
                        W(e, "pe", vmm[dvg][tc])
                        pv = vsp[dvg][tc - 4] if tc >= 4 else \
                            (vsp[dvg - 1][NCH - 4 + tc] if dvg >= 1 else 0)
                        if pv:
                            W(e, "vs%d" % (tc % 4), pv)
                        ins = e.tensor_copy(out=vstag[tc % 4][:, :],
                                            in_=pb[tc % 4][:, :])
                        vev[dvg][tc] = I(ins, "dv", 1)

                    @blk.sync
                    def _(e, dvg=dvg, tc=tc):
                        W(e, "dv", vev[dvg][tc])
                        vsp[dvg][tc] = dma(e, "vs%d" % (tc % 4),
                                           sp_v[tc, :, dvg * G:(dvg + 1) * G],
                                           vstag[tc % 4][:, :])

            w2row = s1.enter_context(nc.sbuf_tensor("w2row", [128, HF], BF16))
            w2st = s1.enter_context(nc.sbuf_tensor("w2st", [128, AH * 128], BF16))
            w2rr = ap_w2.rearrange("(i p) h -> i p h", p=128)
            w2l = [0] * KD
            w2mm = [[0] * KD for _ in range(KD)]
            w2ev = [[0] * KD for _ in range(KD)]
            w2sp = [0] * KD
            for i in range(KD):
                @blk.gpsimd
                def _(e, i=i):
                    if i >= 1:
                        W(e, "pe", w2mm[i - 1][KD - 1])
                    w2l[i] = dma(e, "w2r", w2row[:, :], w2rr[i])

                for half in range(2):
                    @blk.tensor
                    def _(e, i=i, half=half):
                        if half == 0:
                            W(e, "w2r", w2l[i])
                            if i >= 1:
                                W(e, "w2s", w2sp[i - 1])
                        for q4 in range(half * 8, half * 8 + 8):
                            bk = q4 % 8
                            if i == 0 and half == 0:
                                if bk < 4:
                                    W(e, "dv", vev[NG - 1][NCH - 1])
                            else:
                                pv = w2ev[i][q4 - 8] if q4 >= 8 else \
                                    w2ev[i - 1][KD - 8 + q4]
                                W(e, "dv", pv)
                            ins = None
                            for s4 in range(4):
                                a = q4 * 4 + s4
                                ins = e.matmul(
                                    pb[bk][:, s4 * 128:(s4 + 1) * 128],
                                    w2row[:, a * 128:(a + 1) * 128],
                                    idb[:, :], start=True, stop=True)
                            w2mm[i][q4] = I(ins, "pe", 1)

                    @blk.vector
                    def _(e, i=i, half=half):
                        for q4 in range(half * 8, half * 8 + 8):
                            W(e, "pe", w2mm[i][q4])
                            ins = e.tensor_copy(
                                out=w2st[:, q4 * 512:(q4 + 1) * 512],
                                in_=pb[q4 % 8][:, :])
                            w2ev[i][q4] = I(ins, "dv", 1)

                @blk.sync
                def _(e, i=i):
                    W(e, "dv", w2ev[i][KD - 1])
                    w2sp[i] = dma(
                        e, "w2s",
                        sp_w2t[:, :, i * 128:(i + 1) * 128].transpose([1, 0, 2]),
                        w2st[:, :].rearrange("p (a c) -> p a c", a=AH))

        barrier()

        with ExitStack() as s2:
            Sf = s2.enter_context(nc.sbuf_tensor("Sf", [128, H * DK], F32))
            Sb = s2.enter_context(nc.sbuf_tensor("Sb", [128, H * DK], BF16))
            bpre = s2.enter_context(nc.sbuf_tensor("bpre", [128, H], F32))
            epre = s2.enter_context(nc.sbuf_tensor("epre", [128, H], F32))
            dcol = s2.enter_context(nc.sbuf_tensor("dcol", [128, H], F32))
            bhat = s2.enter_context(nc.sbuf_tensor("bhat", [128, H], F32))
            agbuf = s2.enter_context(nc.sbuf_tensor("agbuf", [128, AGW], F32))
            s2a = s2.enter_context(ExitStack())
            qg = s2a.enter_context(nc.sbuf_tensor("qg", [128, H * G], BF16))
            kg = s2a.enter_context(nc.sbuf_tensor("kg", [128, H * G], BF16))
            gg = s2a.enter_context(nc.sbuf_tensor("gg", [128, H * G], BF16))
            vg = s2a.enter_context(nc.sbuf_tensor("vg", [128, 4 * D], BF16))
            sB = s2a.enter_context(nc.sbuf_tensor("sB", [128, D], F32))
            Braw = s2a.enter_context(nc.sbuf_tensor("Braw", [128, D], F32))
            sc1 = s2a.enter_context(nc.sbuf_tensor("sc1", [128, D], F32))
            sc2 = s2a.enter_context(nc.sbuf_tensor("sc2", [128, D], F32))
            eqb = s2a.enter_context(nc.sbuf_tensor("eqb", [128, D], BF16))
            enb = s2a.enter_context(nc.sbuf_tensor("enb", [128, D], BF16))
            qtb = s2a.enter_context(nc.sbuf_tensor("qtb", [128, D], BF16))
            ktb = s2a.enter_context(nc.sbuf_tensor("ktb", [128, D], BF16))
            khb = s2a.enter_context(nc.sbuf_tensor("khb", [128, D], BF16))
            amb = s2a.enter_context(nc.sbuf_tensor("amb", [128, D], BF16))
            ktm = s2a.enter_context(nc.sbuf_tensor("ktm", [128, D], BF16))
            qdst = s2a.enter_context(nc.sbuf_tensor("qdst", [128, H * G], BF16))
            ost = s2a.enter_context(nc.sbuf_tensor("ost", [128, H * G], F32))

            @blk.vector
            def _(e):
                e.memset(Sf[:, :], 0.0)
                e.memset(Sb[:, :], 0.0)
                ins = e.memset(bpre[:, :], 0.0)
                I(ins, "dv", 1)
            p2_init = n["dv"]

            def gv(sp, g):
                return sp[:, :, g * G:(g + 1) * G].transpose([1, 0, 2])

            gld = [0] * NG
            flush_done = [0] * NG
            ls_done = [0] * NCH
            scan_done = [0] * NCH
            exps_done = [0] * NCH
            fac_done = [0] * NCH
            akt_done = [0] * NCH
            am_done = [0] * NCH
            last_pe = [0] * NCH
            last_dv = [0] * NCH

            Bv = Braw[:, :].rearrange("p (h t) -> p h t", h=H)
            bcv = Bv[:, :, 127:128].squeeze(2)

            for g in range(NG):
                @blk.sync
                def _(e, g=g):
                    if g >= 1:
                        W(e, "ac", ls_done[g * 4 - 1])
                        W(e, "dv", fac_done[g * 4 - 1])
                        W(e, "pe", last_pe[g * 4 - 1])
                    dma(e, "g2l", qg[:, :].rearrange("p (j t) -> p j t", j=H),
                        gv(sp_q, g))
                    dma(e, "g2l", kg[:, :].rearrange("p (j t) -> p j t", j=H),
                        gv(sp_k, g))
                    dma(e, "g2l", gg[:, :].rearrange("p (j t) -> p j t", j=H),
                        gv(sp_g, g))
                    gld[g] = dma(e, "g2l",
                                 vg[:, :].rearrange("p (c d) -> p c d", c=4),
                                 sp_v[g * 4:(g + 1) * 4].transpose([1, 0, 2]))

                for ci in range(4):
                    t = g * 4 + ci

                    def chs(buf, ci=ci):
                        return buf[:, :].rearrange("p (j t) -> p j t", j=H)[
                            :, :, ci * 128:(ci + 1) * 128]

                    @blk.scalar
                    def _(e, g=g, ci=ci, t=t):
                        W(e, "g2l", gld[g])
                        if t == 0:
                            W(e, "dv", p2_init)
                        else:
                            W(e, "dv", fac_done[t - 1])
                        gsl = chs(gg)
                        v1 = sc1[:, :].rearrange("p (j t) -> p j t", j=H)
                        v2 = sc2[:, :].rearrange("p (j t) -> p j t", j=H)
                        FE(e, "ac", e.activation(v1, gsl, AF.Abs))
                        FE(e, "ac", e.activation(v2, v1, AF.Exp, scale=-1.0))
                        FE(e, "ac", e.activation(v1, v2, AF.Ln, bias=1.0))
                        e.activation(v2, gsl, AF.Relu, scale=-1.0)
                        ins = e.activation(epre[:, :], bpre[:, :], AF.Exp,
                                           scale=-1.0 / 16.0)
                        ls_done[t] = I(ins, "ac", 1)

                    @blk.vector
                    def _(e, g=g, t=t):
                        W(e, "ac", ls_done[t])
                        if t >= 1:
                            W(e, "ac", exps_done[t - 1])
                        if g >= 1 and t % 4 == 0:
                            W(e, "fl", flush_done[g - 1])
                        FE(e, "dv", e.tensor_add(sB[:, :], sc1[:, :],
                                                 sc2[:, :]))
                        ins = None
                        for h in range(H):
                            ins = e.tensor_tensor_scan(
                                out=Braw[:, h * 128:(h + 1) * 128],
                                data0=sB[:, h * 128:(h + 1) * 128],
                                data1=zer[:, :], initial=0.0,
                                op0=ALU.add, op1=ALU.add)
                        scan_fence = I(ins, "dv", 1)
                        e.wait_ge(sem["dv"], scan_fence)
                        ins = e.tensor_add(bpre[:, :], bpre[:, :], bcv)
                        scan_done[t] = I(ins, "dv", 1)

                    @blk.scalar
                    def _(e, t=t):
                        W(e, "dv", scan_done[t])
                        if t >= 1:
                            W(e, "dv", last_dv[t - 1])
                        e.activation(eqb[:, :], Braw[:, :], AF.Exp,
                                     scale=-1.0 / 16.0, bias=-LOG_SQRT_DK)
                        e.activation(enb[:, :], Braw[:, :], AF.Exp,
                                     scale=1.0 / 16.0)
                        ins = e.activation(dcol[:, :], bcv, AF.Exp,
                                           scale=-1.0 / 16.0)
                        exps_done[t] = I(ins, "ac", 1)

                    @blk.vector
                    def _(e, g=g, ci=ci, t=t):
                        W(e, "ac", exps_done[t])
                        if t >= 1:
                            W(e, "pe", last_pe[t - 1])
                        ev = lambda b: b[:, :].rearrange("p (j t) -> p j t", j=H)
                        e.tensor_mul(ev(qtb), chs(qg), ev(eqb))
                        FE(e, "dv", e.tensor_mul(ev(ktb), chs(kg), ev(enb)))
                        ins = None
                        for h in range(H):
                            ins = e.tensor_scalar_mul(
                                out=khb[:, h * 128:(h + 1) * 128],
                                in0=ktb[:, h * 128:(h + 1) * 128],
                                scalar1=dcol[:, h:h + 1])
                        FE(e, "dv", ins)
                        ins = None
                        for h in range(H):
                            ins = e.tensor_scalar_mul(
                                out=qdst[:, h * G + ci * 128:
                                         h * G + (ci + 1) * 128],
                                in0=qtb[:, h * 128:(h + 1) * 128],
                                scalar1=epre[:, h:h + 1])
                        fac_done[t] = I(ins, "dv", 1)

                    @blk.tensor
                    def _(e, t=t):
                        W(e, "dv", fac_done[t])
                        ins = None
                        for h in range(H):
                            ins = e.matmul(
                                pb[h // 4][:, (h % 4) * 128:(h % 4 + 1) * 128],
                                ktb[:, h * 128:(h + 1) * 128],
                                qtb[:, h * 128:(h + 1) * 128],
                                start=True, stop=True)
                        for h in range(H):
                            ins = e.matmul(
                                pb[4 + h // 4][:, (h % 4) * 128:(h % 4 + 1) * 128],
                                khb[:, h * 128:(h + 1) * 128],
                                idb[:, :], start=True, stop=True)
                        akt_done[t] = I(ins, "pe", 1)

                    @blk.vector
                    def _(e, t=t):
                        W(e, "pe", akt_done[t])
                        if t == 0:
                            W(e, "c_ld", 96)
                        for h in range(H):
                            e.tensor_tensor(
                                out=amb[:, h * 128:(h + 1) * 128],
                                in0=pb[h // 4][:, (h % 4) * 128:(h % 4 + 1) * 128],
                                in1=maskut[:, :], op=ALU.mult)
                        ins = None
                        for h in range(H):
                            ins = e.tensor_copy(
                                out=ktm[:, h * 128:(h + 1) * 128],
                                in_=pb[4 + h // 4][:, (h % 4) * 128:
                                                   (h % 4 + 1) * 128])
                        am_done[t] = I(ins, "dv", 1)

                    @blk.tensor
                    def _(e, ci=ci, t=t):
                        W(e, "dv", am_done[t])
                        ins = None
                        for h in range(H):
                            e.matmul(
                                pb[h // 4][:, (h % 4) * 128:(h % 4 + 1) * 128],
                                Sb[:, h * 128:(h + 1) * 128],
                                qtb[:, h * 128:(h + 1) * 128],
                                start=True, stop=False)
                            ins = e.matmul(
                                pb[h // 4][:, (h % 4) * 128:(h % 4 + 1) * 128],
                                vg[:, ci * D + h * 128:ci * D + (h + 1) * 128],
                                amb[:, h * 128:(h + 1) * 128],
                                start=False, stop=True)
                        I(ins, "pe", 1)
                        for h in range(H):
                            ins = e.matmul(
                                pb[4 + h // 4][:, (h % 4) * 128:(h % 4 + 1) * 128],
                                ktm[:, h * 128:(h + 1) * 128],
                                vg[:, ci * D + h * 128:ci * D + (h + 1) * 128],
                                start=True, stop=True)
                        last_pe[t] = I(ins, "pe", 1)

                    @blk.vector
                    def _(e, ci=ci, t=t):
                        W(e, "pe", last_pe[t])
                        for h in range(H):
                            e.tensor_copy(
                                out=ost[:, h * G + ci * 128:h * G + (ci + 1) * 128],
                                in_=pb[h // 4][:, (h % 4) * 128:(h % 4 + 1) * 128])
                        for h in range(H):
                            ins = e.scalar_tensor_tensor(
                                out=Sf[:, h * 128:(h + 1) * 128],
                                in0=Sf[:, h * 128:(h + 1) * 128],
                                scalar=dcol[:, h:h + 1],
                                in1=pb[4 + h // 4][:, (h % 4) * 128:
                                                   (h % 4 + 1) * 128],
                                op0=ALU.mult, op1=ALU.add)
                        FE(e, "dv", ins)
                        ins = e.tensor_copy(out=Sb[:, :], in_=Sf[:, :])
                        last_dv[t] = I(ins, "dv", 1)

                if dump:
                    for ci in range(4):
                        @blk.sync
                        def _(e, g=g, ci=ci):
                            W(e, "dv", last_dv[g * 4 + ci])
                            dma(e, "dbg", sp_sdbg[g * 4 + ci], Sf[:, :])
                            if g == 0 and ci == 0:
                                dma(e, "dbg", sp_kdbg[0], ktm[:, :].bitcast(F32))
                                dma(e, "dbg", sp_kdbg[1], khb[:, :].bitcast(F32))

                @blk.sync
                def _(e, g=g):
                    W(e, "dv", last_dv[g * 4 + 3])
                    dma(e, "fl", gv(sp_qd, g),
                        qdst[:, :].rearrange("p (j t) -> p j t", j=H))
                    flush_done[g] = dma(e, "fl", gv(sp_o, g),
                                        ost[:, :].rearrange("p (j t) -> p j t", j=H))

            s2a.close()

            @blk.scalar
            def _(e):
                W(e, "dv", last_dv[NCH - 1])
                ins = e.activation(agbuf[:, H * DK:], bpre[:, :], AF.Exp,
                                   scale=-1.0 / 16.0)
                I(ins, "ac", 1)
            dtot_ac = n["ac"]

            @blk.vector
            def _(e):
                W(e, "ac", dtot_ac)
                ins = e.tensor_copy(out=agbuf[:, :H * DK], in_=Sf[:, :])
                I(ins, "dv", 1)
            agpack_dv = n["dv"]

            @blk.gpsimd
            def _(e):
                W(e, "dv", agpack_dv)
                v_agi = dma(e, "agi", ag_in.ap(), agbuf[:, :])
                W(e, "agi", v_agi)
                e.collective_compute(
                    "AllGather", ALU.bypass,
                    replica_groups=[list(range(NCORE))],
                    ins=[ag_in.ap().opt()],
                    outs=[ag_out.ap().opt()],
                ).then_inc(sem["cc"], 1)
                n["cc"] += 1
            cc_done = n["cc"]

            with ExitStack() as s2b:
                agr = s2b.enter_context(nc.sbuf_tensor("agr", [128, 7 * AGW], F32))
                tmsk = s2b.enter_context(nc.sbuf_tensor("tmsk", [128, H * DK], F32))
                sinf = s2b.enter_context(nc.sbuf_tensor("sinf", [128, H * DK], F32))
                dhat = s2b.enter_context(nc.sbuf_tensor("dhat", [128, H], F32))

                @blk.sync
                def _(e):
                    W(e, "cc", cc_done)
                    W(e, "fl", flush_done[NG - 1])
                    agv = ag_out.ap().rearrange("(r p) w -> r p w", p=128)
                    dma(e, "agr", agr[:, :].rearrange("p (r w) -> p r w", r=7),
                        agv[0:7].transpose([1, 0, 2]))
                agr_ld = n["agr"]

                @blk.vector
                def _(e):
                    W(e, "agr", agr_ld)
                    e.memset(sinf[:, :], 0.0)
                    for b in range(7):
                        base = b * AGW
                        e.tensor_scalar(out=dhat[:, :],
                                        in0=agr[:, base + H * DK:base + AGW],
                                        scalar1=cmask[:, b:b + 1],
                                        scalar2=icmask[:, b:b + 1],
                                        op0=ALU.mult, op1=ALU.add)
                        FE(e, "dv", e.tensor_scalar_mul(
                            out=tmsk[:, :],
                            in0=agr[:, base:base + H * DK],
                            scalar1=cmask[:, b:b + 1]))
                        for h in range(H):
                            ins = e.scalar_tensor_tensor(
                                out=sinf[:, h * 128:(h + 1) * 128],
                                in0=sinf[:, h * 128:(h + 1) * 128],
                                scalar=dhat[:, h:h + 1],
                                in1=tmsk[:, h * 128:(h + 1) * 128],
                                op0=ALU.mult, op1=ALU.add)
                        FE(e, "dv", ins)
                    ins = e.tensor_copy(out=sin_bf[:, :], in_=sinf[:, :])
                    I(ins, "dv", 1)
        sin_done = n["dv"]

        barrier()

        with ExitStack() as s3:
            xnf2 = s3.enter_context(nc.sbuf_tensor("xnf2", [128, KD * T], BF16))

            with ExitStack() as s3a:
                wob = s3a.enter_context(nc.sbuf_tensor("wob", [128, KD * D], BF16))
                obuf = s3a.enter_context(nc.sbuf_tensor("obuf", [128, 4 * G], F32))
                qdb = s3a.enter_context(nc.sbuf_tensor("qdb", [128, 4 * G], BF16))
                ofull = s3a.enter_context(nc.sbuf_tensor("ofull", [128, H * G], BF16))
                abuf = s3a.enter_context(nc.sbuf_tensor("abuf", [128, D], F32))
                sqd = s3a.enter_context(nc.sbuf_tensor("sqd", [128, 1024], F32))
                xre = s3a.enter_context(nc.sbuf_tensor("xre", [128, D], F32))
                r2t = s3a.enter_context(nc.sbuf_tensor("r2t", [128, D], F32))
                scr3 = s3a.enter_context(nc.sbuf_tensor("scr3", [128, D], F32))
                xnb3 = s3a.enter_context(nc.sbuf_tensor("xnb3", [128, D], BF16))

                wor = ap_w["wo"].rearrange("(k p) d -> k p d", p=128)

                @blk.gpsimd
                def _(e):
                    for kk in range(KD):
                        dma(e, "wo", wob[:, kk * D:(kk + 1) * D], wor[kk])
                wo_ld = n["wo"]

                corr_pe = [0] * 8
                corr_add_dv = [0] * 4
                corr_dv_last = 0
                wo_last_pe = 0
                wo_pe = [0] * 4
                wo_ev = [0] * 4
                ln3s_prev = 0
                r2sp_prev = 0
                xn2t_pe = [0] * NCH
                xn2g_pe = [0] * 4
                xn2ev_dv = [[0] * 4 for _ in range(NCH)]
                xn2ev_last = 0
                half_ld_box = [0]
                xre_ld_box = [0]
                r2sp_box = [0]

                for g in range(NG):
                    for hb in range(4):
                        @blk.sync
                        def _(e, g=g, hb=hb):
                            if corr_dv_last:
                                W(e, "dv", corr_dv_last)
                            dma(e, "oq",
                                obuf[:, :].rearrange("p (j t) -> p j t", j=4),
                                sp_o[hb * 4:(hb + 1) * 4, :, g * G:(g + 1) * G]
                                .transpose([1, 0, 2]))
                            half_ld_box[0] = dma(
                                e, "oq",
                                qdb[:, :].rearrange("p (j t) -> p j t", j=4),
                                sp_qd[hb * 4:(hb + 1) * 4, :, g * G:(g + 1) * G]
                                .transpose([1, 0, 2]))

                        @blk.tensor
                        def _(e, g=g, hb=hb):
                            W(e, "oq", half_ld_box[0])
                            if g == 0 and hb == 0:
                                W(e, "dv", sin_done)
                            if g >= 1 and hb == 0:
                                W(e, "dv", xn2ev_last)
                            for hh in range(4):
                                h = hb * 4 + hh
                                if corr_add_dv[hh]:
                                    W(e, "dv", corr_add_dv[hh])
                                ins = e.matmul(pb[hh][:, :],
                                               sin_bf[:, h * 128:(h + 1) * 128],
                                               qdb[:, hh * G:(hh + 1) * G],
                                               start=True, stop=True)
                                corr_pe[hh] = I(ins, "pe", 1)

                        @blk.vector
                        def _(e, g=g, hb=hb):
                            if wo_last_pe and hb == 0:
                                W(e, "pe", wo_last_pe)
                            for hh in range(4):
                                h = hb * 4 + hh
                                W(e, "pe", corr_pe[hh])
                                ins = e.tensor_tensor(
                                    out=ofull[:, h * G:(h + 1) * G],
                                    in0=pb[hh][:, :],
                                    in1=obuf[:, hh * G:(hh + 1) * G],
                                    op=ALU.add)
                                corr_add_dv[hh] = I(ins, "dv", 1)
                        corr_dv_last = n["dv"]

                    corr_done_g = corr_dv_last

                    for tc4 in range(4):
                        t = g * 4 + tc4

                        @blk.sync
                        def _(e, t=t):
                            if ln3s_prev:
                                W(e, "dv", ln3s_prev)
                            xre_ld_box[0] = dma(e, "xr", xre[:, :], xr[t])

                        @blk.tensor
                        def _(e, g=g, tc4=tc4):
                            W(e, "dv", corr_done_g)
                            W(e, "wo", wo_ld)
                            for dg in range(4):
                                if wo_ev[dg]:
                                    W(e, "ac", wo_ev[dg])
                                ins = None
                                for kk in range(KD):
                                    ins = e.matmul(
                                        pb[4 + dg][:, :],
                                        ofull[:, kk * G + tc4 * 128:
                                              kk * G + tc4 * 128 + 128],
                                        wob[:, kk * D + dg * 512:
                                            kk * D + (dg + 1) * 512],
                                        start=(kk == 0), stop=(kk == KD - 1))
                                wo_pe[dg] = I(ins, "pe", 1)
                        wo_last_pe = n["pe"]

                        @blk.scalar
                        def _(e):
                            if ln3s_prev:
                                W(e, "dv", ln3s_prev)
                            for dg in range(4):
                                W(e, "pe", wo_pe[dg])
                                I(e.activation(abuf[:, dg * 512:(dg + 1) * 512],
                                               pb[4 + dg][:, :], AF.Copy,
                                               accum_out=stc[:, 16 + dg:17 + dg]),
                                  "ac", 1)
                                ins = e.activation(
                                    sqd[:, (dg % 2) * 512:(dg % 2) * 512 + 512],
                                    pb[4 + dg][:, :], AF.Square,
                                    accum_out=stc[:, 20 + dg:21 + dg])
                                wo_ev[dg] = I(ins, "ac", 1)
                                e.wait_ge(sem["ac"], wo_ev[dg])

                        @blk.vector
                        def _(e):
                            W(e, "ac", wo_ev[3])
                            e.reduce_sum(stc[:, 24:25], stc[:, 16:20], axis=AXX)
                            ins = e.reduce_sum(stc[:, 25:26], stc[:, 20:24],
                                               axis=AXX)
                            I(ins, "dv", 1)
                        ln2s_dv = n["dv"]

                        @blk.scalar
                        def _(e):
                            W(e, "dv", ln2s_dv)
                            FE(e, "ac", e.mul(stc[:, 24:25], stc[:, 24:25],
                                              1.0 / D))
                            FE(e, "ac", e.activation(stc[:, 26:27],
                                                     stc[:, 24:25], AF.Square))
                            FE(e, "ac", e.mul(stc[:, 26:27], stc[:, 26:27],
                                              -1.0))
                            FE(e, "ac", e.activation(stc[:, 27:28],
                                                     stc[:, 25:26], AF.Identity,
                                                     bias=stc[:, 26:27],
                                                     scale=1.0 / D))
                            FE(e, "ac", e.activation(stc[:, 27:28],
                                                     stc[:, 27:28], AF.Ln,
                                                     bias=EPS))
                            ins = e.activation(stc[:, 27:28], stc[:, 27:28],
                                               AF.Exp, scale=-0.5)
                            I(ins, "ac", 1)
                        ln2r_ac = n["ac"]

                        @blk.vector
                        def _(e):
                            W(e, "ac", ln2r_ac)
                            W(e, "xr", xre_ld_box[0])
                            if r2sp_prev:
                                W(e, "r2s", r2sp_prev)
                            FE(e, "dv", e.tensor_scalar(
                                out=scr3[:, :], in0=abuf[:, :],
                                scalar1=stc[:, 24:25], scalar2=stc[:, 27:28],
                                op0=ALU.subtract, op1=ALU.mult))
                            FE(e, "dv", e.tensor_add(r2t[:, :], scr3[:, :],
                                                     xre[:, :]))
                            FE(e, "dv", e.tensor_scalar(
                                out=scr3[:, :], in0=r2t[:, :],
                                scalar1=1.0, scalar2=0.0,
                                op0=ALU.mult, op1=ALU.add,
                                accum_out=stc[:, 28:29]))
                            ins = e.scalar_tensor_tensor(
                                out=scr3[:, :], in0=r2t[:, :], scalar=1.0,
                                in1=r2t[:, :], op0=ALU.bypass, op1=ALU.mult,
                                accum_out=stc[:, 29:30])
                            I(ins, "dv", 1)
                        ln3s_dv = n["dv"]
                        ln3s_prev = ln3s_dv

                        @blk.sync
                        def _(e, t=t):
                            W(e, "dv", ln3s_dv)
                            r2sp_box[0] = dma(e, "r2s", sp_r2[t], r2t[:, :])
                        r2sp_prev = r2sp_box[0]

                        @blk.scalar
                        def _(e):
                            W(e, "dv", ln3s_dv)
                            FE(e, "ac", e.mul(stc[:, 28:29], stc[:, 28:29],
                                              1.0 / D))
                            FE(e, "ac", e.activation(stc[:, 30:31],
                                                     stc[:, 28:29], AF.Square))
                            FE(e, "ac", e.mul(stc[:, 30:31], stc[:, 30:31],
                                              -1.0))
                            FE(e, "ac", e.activation(stc[:, 31:32],
                                                     stc[:, 29:30], AF.Identity,
                                                     bias=stc[:, 30:31],
                                                     scale=1.0 / D))
                            FE(e, "ac", e.activation(stc[:, 31:32],
                                                     stc[:, 31:32], AF.Ln,
                                                     bias=EPS))
                            ins = e.activation(stc[:, 31:32], stc[:, 31:32],
                                               AF.Exp, scale=-0.5)
                            I(ins, "ac", 1)
                        ln3r_ac = n["ac"]

                        @blk.vector
                        def _(e, t=t):
                            W(e, "ac", ln3r_ac)
                            if t >= 1:
                                W(e, "pe", xn2t_pe[t - 1])
                            ins = e.tensor_scalar(
                                out=xnb3[:, :], in0=r2t[:, :],
                                scalar1=stc[:, 28:29], scalar2=stc[:, 31:32],
                                op0=ALU.subtract, op1=ALU.mult)
                            I(ins, "dv", 1)
                        xn3_dv = n["dv"]

                        @blk.tensor
                        def _(e, t=t):
                            W(e, "dv", xn3_dv)
                            for grp in range(4):
                                if t >= 1:
                                    W(e, "dv", xn2ev_dv[t - 1][grp])
                                ins = None
                                for d4 in range(4):
                                    dj = grp * 4 + d4
                                    ins = e.matmul(
                                        pb[grp][:, d4 * 128:(d4 + 1) * 128],
                                        xnb3[:, dj * 128:(dj + 1) * 128],
                                        idb[:, :], start=True, stop=True)
                                xn2g_pe[grp] = I(ins, "pe", 1)
                            xn2t_pe[t] = n["pe"]

                        @blk.vector
                        def _(e, t=t):
                            for grp in range(4):
                                W(e, "pe", xn2g_pe[grp])
                                dst = xnf2[:, :].rearrange(
                                    "p (j c) -> p j c", j=KD)[
                                    :, grp * 4:(grp + 1) * 4,
                                    t * 128:(t + 1) * 128]
                                ins = e.tensor_copy(
                                    out=dst,
                                    in_=pb[grp][:, :].rearrange(
                                        "p (j c) -> p j c", j=4))
                                xn2ev_dv[t][grp] = I(ins, "dv", 1)
                    xn2ev_last = xn2ev_dv[g * 4 + 3][3]
                xnf2_done = xn2ev_dv[NCH - 1][3]

            barrier()

            with ExitStack() as s4:
                w1row = [s4.enter_context(nc.sbuf_tensor(f"w1r{i}", [128, D], BF16))
                         for i in range(2)]
                w3row = [s4.enter_context(nc.sbuf_tensor(f"w3r{i}", [128, D], BF16))
                         for i in range(2)]
                w1T = [s4.enter_context(
                    nc.sbuf_tensor(f"w1T{i}", [128, KD * 128], BF16))
                    for i in range(2)]
                w3T = [s4.enter_context(
                    nc.sbuf_tensor(f"w3T{i}", [128, KD * 128], BF16))
                    for i in range(2)]
                silus = s4.enter_context(nc.sbuf_tensor("silus", [128, G], F32))
                yst = [s4.enter_context(nc.sbuf_tensor(f"yst{i}", [128, T], BF16))
                       for i in range(2)]
                w1r = ap_w1.rearrange("(j p) d -> j p d", p=128)
                w3r = ap_w3.rearrange("(j p) d -> j p d", p=128)

                wrow_ld = [0] * AH
                wT_pe = [0] * AH
                wT_dv = [0] * AH
                up_pe = [[0] * 8 for _ in range(AH)]
                sil_ac = [0] * NG
                y_dv = [[0] * NG for _ in range(AH)]
                ysp_sq = [0] * AH

                for j in range(AH):
                    @blk.gpsimd
                    def _(e, j=j):
                        if j >= 2:
                            W(e, "pe", wT_pe[j - 2])
                        dma(e, "p4w%d" % (j % 2), w1row[j % 2][:, :], w1r[j])
                        wrow_ld[j] = dma(e, "p4w%d" % (j % 2),
                                         w3row[j % 2][:, :], w3r[j])

                    @blk.tensor
                    def _(e, j=j):
                        W(e, "p4w%d" % (j % 2), wrow_ld[j])
                        if j == 0:
                            pass
                        else:
                            W(e, "dv", wT_dv[j - 1])
                            W(e, "dv", y_dv[j - 1][NG - 1])
                        ins = None
                        for i in range(KD):
                            ins = e.matmul(
                                pb[i // 4][:, (i % 4) * 128:(i % 4 + 1) * 128],
                                w1row[j % 2][:, i * 128:(i + 1) * 128],
                                idb[:, :], start=True, stop=True)
                        for i in range(KD):
                            ins = e.matmul(
                                pb[4 + i // 4][:, (i % 4) * 128:(i % 4 + 1) * 128],
                                w3row[j % 2][:, i * 128:(i + 1) * 128],
                                idb[:, :], start=True, stop=True)
                        wT_pe[j] = I(ins, "pe", 1)

                    @blk.vector
                    def _(e, j=j):
                        W(e, "pe", wT_pe[j])
                        ins = None
                        for b in range(4):
                            e.tensor_copy(out=w1T[j % 2][:, b * 512:(b + 1) * 512],
                                          in_=pb[b][:, :])
                            ins = e.tensor_copy(
                                out=w3T[j % 2][:, b * 512:(b + 1) * 512],
                                in_=pb[4 + b][:, :])
                        wT_dv[j] = I(ins, "dv", 1)

                    for g in range(NG):
                        @blk.tensor
                        def _(e, j=j, g=g):
                            if g == 0:
                                W(e, "dv", wT_dv[j])
                            if j >= 1:
                                W(e, "dv", y_dv[j - 1][g])
                            ins = None
                            for kk in range(KD):
                                ins = e.matmul(
                                    pb[g][:, :],
                                    w1T[j % 2][:, kk * 128:(kk + 1) * 128],
                                    xnf2[:, kk * T + g * G:kk * T + (g + 1) * G],
                                    start=(kk == 0), stop=(kk == KD - 1))
                            up_pe[j][g * 2] = I(ins, "pe", 1)
                            for kk in range(KD):
                                ins = e.matmul(
                                    pb[4 + g][:, :],
                                    w3T[j % 2][:, kk * 128:(kk + 1) * 128],
                                    xnf2[:, kk * T + g * G:kk * T + (g + 1) * G],
                                    start=(kk == 0), stop=(kk == KD - 1))
                            up_pe[j][g * 2 + 1] = I(ins, "pe", 1)

                        @blk.scalar
                        def _(e, j=j, g=g):
                            W(e, "pe", up_pe[j][g * 2])
                            if j >= 1 or g >= 1:
                                pj, pg = (j, g - 1) if g >= 1 else (j - 1, NG - 1)
                                W(e, "dv", y_dv[pj][pg])
                            ins = e.activation(silus[:, :], pb[g][:, :], AF.Silu)
                            sil_ac[g] = I(ins, "ac", 1)

                        @blk.vector
                        def _(e, j=j, g=g):
                            W(e, "ac", sil_ac[g])
                            W(e, "pe", up_pe[j][g * 2 + 1])
                            if j >= 2 and g == 0:
                                W(e, "ys%d" % (j % 2), ysp_sq[j - 2])
                            ins = e.scalar_tensor_tensor(
                                out=yst[j % 2][:, g * G:(g + 1) * G],
                                in0=silus[:, :], scalar=1.0, in1=pb[4 + g][:, :],
                                op0=ALU.bypass, op1=ALU.mult)
                            y_dv[j][g] = I(ins, "dv", 1)

                    @blk.sync
                    def _(e, j=j):
                        W(e, "dv", y_dv[j][NG - 1])
                        ysp_sq[j] = dma(e, "ys%d" % (j % 2), sp_h[j],
                                        yst[j % 2][:, :])

        barrier()

        with ExitStack() as s5:
            Yb = s5.enter_context(nc.sbuf_tensor("Yb", [128, AH * G], BF16))
            Zb = s5.enter_context(nc.sbuf_tensor("Zb", [128, (G // 2) * 128], BF16))
            Asb = [s5.enter_context(nc.sbuf_tensor(f"As{i}", [128, 512], BF16))
                   for i in range(2)]
            Tsb = [s5.enter_context(nc.sbuf_tensor(f"Ts{i}", [128, 512], BF16))
                   for i in range(2)]
            w2rg = s5.enter_context(nc.sbuf_tensor("w2rg", [128, 16 * 512], BF16))
            scr5 = s5.enter_context(nc.sbuf_tensor("scr5", [128, 1024], F32))
            fbuf = [s5.enter_context(nc.sbuf_tensor(f"fb{i}", [128, D], F32))
                    for i in range(4)]
            r2re = [s5.enter_context(nc.sbuf_tensor(f"r2r{i}", [128, D], F32))
                    for i in range(2)]

            yld = [0] * NG
            r2ld = [[0] * 4 for _ in range(NG)]
            out_sq = [[0] * 4 for _ in range(NG)]
            out_dv = [[0] * 4 for _ in range(NG)]
            f_ac = [[0] * 4 for _ in range(4)]
            w2mm_g = [0] * 32
            w2_last_pe = 0
            Yv = Yb[:, :].rearrange("p (a t) -> p a t", a=AH)

            mm1_last_prev = [0]
            for g5 in range(NG):
                @blk.sync
                def _(e, g5=g5):
                    if g5 >= 1:
                        W(e, "pe", mm1_last_prev[0])
                    yld[g5] = dma(e, "yl",
                                  Yv, sp_h[:, :, g5 * G:(g5 + 1) * G]
                                  .transpose([1, 0, 2]))

                NQ = G // 8
                mm1_pe = [0] * NQ
                mmT_pe = [0] * NQ
                mm2_pe = [0] * NQ
                evA_dv = [0] * NQ
                evT_dv = [0] * NQ
                evB_dv = [0] * NQ

                for q in range(NQ + 2):
                    @blk.tensor
                    def _(e, g5=g5, q=q):
                        ins = None
                        if 2 <= q < NQ + 2:
                            W(e, "dv", evT_dv[q - 2])
                            for d4 in range(4):
                                ins = e.matmul(
                                    pb[2 + (q - 2) % 2][:, d4 * 128:(d4 + 1) * 128],
                                    Tsb[(q - 2) % 2][:, d4 * 128:(d4 + 1) * 128],
                                    h64i2[:, :], start=True, stop=True)
                            mm2_pe[q - 2] = I(ins, "pe", 1)
                        if 1 <= q < NQ + 1:
                            W(e, "dv", evA_dv[q - 1])
                            if q >= 3:
                                W(e, "dv", evB_dv[q - 3])
                            for d4 in range(4):
                                ins = e.matmul(
                                    pb[2 + (q - 1) % 2][:, d4 * 128:(d4 + 1) * 128],
                                    Asb[(q - 1) % 2][:, d4 * 128:(d4 + 1) * 128],
                                    idb[:, :], start=True, stop=True)
                            mmT_pe[q - 1] = I(ins, "pe", 1)
                        if q < NQ:
                            if q == 0:
                                W(e, "yl", yld[g5])
                                if g5 == 0:
                                    W(e, "c_ld", 96)
                            if q >= 2:
                                W(e, "dv", evA_dv[q - 2])
                            ins = e.matmul(
                                pb[q % 2][:, :], h128s[:, :],
                                Yv[:, :, 8 * q:8 * q + 8],
                                start=True, stop=True)
                            mm1_pe[q] = I(ins, "pe", 1)

                    @blk.vector
                    def _(e, g5=g5, q=q):
                        if 2 <= q < NQ + 2:
                            W(e, "pe", mm2_pe[q - 2])
                            if q == 2 and w2_last_pe:
                                W(e, "pe", w2_last_pe)
                            qq = q - 2
                            src_ap = pb[2 + qq % 2][:, :].rearrange(
                                "p (d c) -> p d c", d=4)
                            dst_ap = Zb[:, :].rearrange(
                                "p (a t) -> p a t", a=AH)[
                                :, :, 8 * qq:8 * qq + 8].rearrange(
                                "p a (d t2) -> p d a t2", d=4)
                            ins = e.tensor_copy(out=dst_ap, in_=src_ap.rearrange(
                                "p d (a t2) -> p d a t2", a=AH))
                            evB_dv[qq] = I(ins, "dv", 1)
                        if 1 <= q < NQ + 1:
                            W(e, "pe", mmT_pe[q - 1])
                            ins = e.tensor_copy(out=Tsb[(q - 1) % 2][:, :],
                                                in_=pb[2 + (q - 1) % 2][:, :])
                            evT_dv[q - 1] = I(ins, "dv", 1)
                        if q < NQ:
                            W(e, "pe", mm1_pe[q])
                            src_ap = pb[q % 2][:, :].rearrange(
                                "p (a d t2) -> p a d t2", a=AH, d=4)
                            dst_ap = Asb[q % 2][:, :].rearrange(
                                "p (d a t2) -> p a d t2", a=AH, d=4)
                            ins = e.tensor_copy(out=dst_ap, in_=src_ap)
                            evA_dv[q] = I(ins, "dv", 1)

                sand_dv = evB_dv[NQ - 1]
                mm1_last_prev[0] = mm1_pe[NQ - 1]

                w2ld = [0] * 32
                for dg in range(4):
                    for ab in range(8):
                        gi = dg * 8 + ab

                        @blk.sync
                        def _(e, g5=g5, dg=dg, ab=ab, gi=gi):
                            if gi >= 2:
                                W(e, "pe", w2mm_g[gi - 2])
                            elif g5 >= 1:
                                W(e, "pe", w2_last_pe)
                            w2ld[gi] = dma(
                                e, "wt%d" % (gi % 2),
                                w2rg[:, (gi % 2) * 4096:(gi % 2) * 4096 + 4096]
                                .rearrange("p (a c) -> p a c", a=8),
                                sp_w2t[ab * 8:(ab + 1) * 8, :,
                                       dg * 512:(dg + 1) * 512]
                                .transpose([1, 0, 2]))

                        @blk.tensor
                        def _(e, g5=g5, dg=dg, ab=ab, gi=gi):
                            W(e, "wt%d" % (gi % 2), w2ld[gi])
                            if ab == 0:
                                W(e, "dv", sand_dv)
                                if dg >= 1:
                                    W(e, "ac", f_ac[dg - 1][3])
                                elif g5 >= 1:
                                    W(e, "ac", f_ac[3][3])
                            ins = None
                            for a8 in range(8):
                                a2 = ab * 8 + a8
                                for tc in range(4):
                                    ins = e.matmul(
                                        pb[4 + tc][:, :],
                                        Zb[:, a2 * G + tc * 128:
                                           a2 * G + (tc + 1) * 128],
                                        w2rg[:, (gi % 2) * 4096 + a8 * 512:
                                             (gi % 2) * 4096 + (a8 + 1) * 512],
                                        start=(a2 == 0), stop=(a2 == AH - 1))
                            w2mm_g[gi] = I(ins, "pe", 1)
                    w2_last_pe = w2mm_g[dg * 8 + 7]

                    @blk.scalar
                    def _(e, g5=g5, dg=dg):
                        W(e, "pe", w2mm_g[dg * 8 + 7])
                        for tc in range(4):
                            if g5 >= 1 and dg == 0:
                                W(e, "ob%d" % tc, out_sq[g5 - 1][tc])
                            I(e.activation(fbuf[tc][:, dg * 512:(dg + 1) * 512],
                                           pb[4 + tc][:, :], AF.Copy,
                                           accum_out=stc[:, 16 + 4 * tc + dg:
                                                         17 + 4 * tc + dg]),
                              "ac", 1)
                            ins = e.activation(
                                scr5[:, (tc % 2) * 512:(tc % 2) * 512 + 512],
                                pb[4 + tc][:, :], AF.Square,
                                accum_out=stc[:, 4 * tc + dg:4 * tc + dg + 1])
                            f_ac[dg][tc] = I(ins, "ac", 1)
                            e.wait_ge(sem["ac"], f_ac[dg][tc])

                for tc in range(4):
                    t = g5 * 4 + tc

                    @blk.sync
                    def _(e, g5=g5, tc=tc, t=t):
                        gp = g5 * 4 + tc - 2
                        if gp >= 0:
                            W(e, "dv", out_dv[gp // 4][gp % 4])
                        r2ld[g5][tc] = dma(e, "r5%d" % (tc % 2),
                                           r2re[tc % 2][:, :], sp_r2[t])

                    @blk.vector
                    def _(e, tc=tc):
                        W(e, "ac", f_ac[3][tc])
                        e.reduce_sum(stc[:, 48:49], stc[:, 16 + 4 * tc:
                                                        20 + 4 * tc], axis=AXX)
                        ins = e.reduce_sum(stc[:, 49:50],
                                           stc[:, 4 * tc:4 * tc + 4], axis=AXX)
                        I(ins, "dv", 1)
                    l4s_dv = n["dv"]

                    @blk.scalar
                    def _(e):
                        W(e, "dv", l4s_dv)
                        FE(e, "ac", e.mul(stc[:, 48:49], stc[:, 48:49],
                                          1.0 / D))
                        FE(e, "ac", e.activation(stc[:, 50:51], stc[:, 48:49],
                                                 AF.Square))
                        FE(e, "ac", e.mul(stc[:, 50:51], stc[:, 50:51], -1.0))
                        FE(e, "ac", e.activation(stc[:, 51:52], stc[:, 49:50],
                                                 AF.Identity,
                                                 bias=stc[:, 50:51],
                                                 scale=1.0 / D))
                        FE(e, "ac", e.activation(stc[:, 51:52], stc[:, 51:52],
                                                 AF.Ln, bias=EPS))
                        ins = e.activation(stc[:, 51:52], stc[:, 51:52],
                                           AF.Exp, scale=-0.5)
                        I(ins, "ac", 1)
                    l4r_ac = n["ac"]

                    @blk.vector
                    def _(e, g5=g5, tc=tc):
                        W(e, "ac", l4r_ac)
                        W(e, "r5%d" % (tc % 2), r2ld[g5][tc])
                        FE(e, "dv", e.tensor_scalar(
                            out=fbuf[tc][:, :], in0=fbuf[tc][:, :],
                            scalar1=stc[:, 48:49], scalar2=stc[:, 51:52],
                            op0=ALU.subtract, op1=ALU.mult))
                        ins = e.tensor_add(fbuf[tc][:, :], fbuf[tc][:, :],
                                           r2re[tc % 2][:, :])
                        out_dv[g5][tc] = I(ins, "dv", 1)

                    @blk.sync
                    def _(e, g5=g5, tc=tc, t=t):
                        W(e, "dv", out_dv[g5][tc])
                        out_sq[g5][tc] = dma(e, "ob%d" % tc, outr[t],
                                             fbuf[tc][:, :])

            @blk.sync
            def _(e):
                for tc in range(4):
                    W(e, "ob%d" % tc, n["ob%d" % tc])

    return nc


def kernel(L0_attn_x, L0_attn_Wq, L0_attn_Wk, L0_attn_Wv, L0_attn_Wg,
           L0_attn_Wo, L0_ffn_W1, L0_ffn_W2, L0_ffn_W3):
    x = np.ascontiguousarray(
        np.asarray(L0_attn_x, dtype=np.float32)).reshape(16384, D)
    ws = {k: np.ascontiguousarray(np.asarray(v, dtype=np.float32)) for k, v in
          (("wq", L0_attn_Wq), ("wk", L0_attn_Wk), ("wv", L0_attn_Wv),
           ("wg", L0_attn_Wg), ("wo", L0_attn_Wo), ("w1", L0_ffn_W1),
           ("w2", L0_ffn_W2), ("w3", L0_ffn_W3))}
    nc = _build()
    in_maps = []
    for c in range(NCORE):
        m = {"x": x[c * T:(c + 1) * T]}
        m.update(ws)
        m.update(_host_consts(c))
        in_maps.append(m)
    res = run_bass_kernel_spmd(nc, in_maps, core_ids=list(range(NCORE)))
    out = np.concatenate([res.results[c]["out"] for c in range(NCORE)], axis=0)
    return out.reshape(1, 16384, D)


# version 18 (final)
# speedup vs baseline: 1.0280x; 1.0280x over previous
import sys
from contextlib import ExitStack

sys.path.insert(0, "/opt/trn_rl_repo")

import numpy as np
import ml_dtypes
import concourse.bass as bass
import concourse.mybir as mybir
from concourse.bass_utils import run_bass_kernel_spmd

F32 = mybir.dt.float32
BF16 = mybir.dt.bfloat16
AF = mybir.ActivationFunctionType
ALU = mybir.AluOpType
AXX = mybir.AxisListType.X

NCORE = 8
D = 2048
T = 2048
NCH = 16
G = 512
NG = 4
H = 16
DK = 128
HF = 8192
AH = 64
KD = 16
EPS = 1e-5
LOG_SQRT_DK = 0.5 * float(np.log(DK))
RT2 = float(np.sqrt(2.0))
AGW = H * DK + H


def _hadamard(nn):
    i = np.arange(nn)
    pc = np.zeros((nn, nn), dtype=np.int64)
    a = i[:, None] & i[None, :]
    while a.any():
        pc += a & 1
        a >>= 1
    return np.where(pc % 2 == 0, 1.0, -1.0).astype(np.float32)


def _host_consts(core_id):
    bf = ml_dtypes.bfloat16
    h128s = (_hadamard(128) * (2.0 ** -7)).astype(bf)
    h64i2 = (np.kron(_hadamard(64), np.eye(2, dtype=np.float32))
             * np.sqrt(2.0)).astype(bf)
    ident = np.eye(128, dtype=np.float32).astype(bf)
    maskut = (np.arange(128)[:, None] <= np.arange(128)[None, :]).astype(np.float32)
    cm = np.zeros((128, 8), dtype=np.float32)
    cm[:, :core_id] = 1.0
    return {
        "c_h128s": h128s, "c_h64i2": h64i2, "c_ident": ident,
        "c_maskut": maskut, "c_cmask": cm,
        "c_icmask": (1.0 - cm).astype(np.float32),
    }


def _build(dump=False):
    nc = bass.Bass("TRN2", target_bir_lowering=False, debug=False)

    ap_x = nc.dram_tensor("x", [T, D], F32, kind="ExternalInput").ap()
    ap_w = {nm: nc.dram_tensor(nm, [D, D], F32, kind="ExternalInput").ap()
            for nm in ("wq", "wk", "wv", "wg", "wo")}
    ap_w1 = nc.dram_tensor("w1", [HF, D], F32, kind="ExternalInput").ap()
    ap_w2 = nc.dram_tensor("w2", [D, HF], F32, kind="ExternalInput").ap()
    ap_w3 = nc.dram_tensor("w3", [HF, D], F32, kind="ExternalInput").ap()
    c_h128s = nc.dram_tensor("c_h128s", [128, 128], BF16, kind="ExternalInput").ap()
    c_h64i2 = nc.dram_tensor("c_h64i2", [128, 128], BF16, kind="ExternalInput").ap()
    c_ident = nc.dram_tensor("c_ident", [128, 128], BF16, kind="ExternalInput").ap()
    c_maskut = nc.dram_tensor("c_maskut", [128, 128], F32, kind="ExternalInput").ap()
    c_cmask = nc.dram_tensor("c_cmask", [128, 8], F32, kind="ExternalInput").ap()
    c_icmask = nc.dram_tensor("c_icmask", [128, 8], F32, kind="ExternalInput").ap()
    ap_out = nc.dram_tensor("out", [T, D], F32, kind="ExternalOutput").ap()

    ks = "ExternalOutput" if dump else "Internal"
    sp_q = nc.dram_tensor("sp_q", [KD, 128, T], BF16, kind=ks).ap()
    sp_k = nc.dram_tensor("sp_k", [KD, 128, T], BF16, kind=ks).ap()
    sp_g = nc.dram_tensor("sp_g", [KD, 128, T], BF16, kind=ks).ap()
    sp_v = nc.dram_tensor("sp_v", [NCH, 128, D], BF16, kind=ks).ap()
    sp_qd = nc.dram_tensor("sp_qd", [KD, 128, T], BF16, kind=ks).ap()
    sp_o = nc.dram_tensor("sp_o", [KD, 128, T], F32, kind=ks).ap()
    sp_r2 = nc.dram_tensor("sp_r2", [NCH, 128, D], F32, kind=ks).ap()
    sp_h = nc.dram_tensor("sp_h", [AH, 128, T], BF16, kind=ks).ap()
    sp_w2t = nc.dram_tensor("sp_w2t", [AH, 128, D], BF16, kind=ks).ap()
    sp_sdbg = nc.dram_tensor("sp_sdbg", [NCH, 128, H * DK], F32, kind=ks).ap()
    sp_kdbg = nc.dram_tensor("sp_kdbg", [2, 128, H * DK // 2], F32, kind=ks).ap()
    ag_in = nc.dram_tensor("ag_in", [128, AGW], F32, kind="Internal")
    ag_out = nc.dram_tensor("ag_out", [NCORE * 128, AGW], F32, kind="Internal",
                            addr_space="Shared")

    xr = ap_x.rearrange("(c p) d -> c p d", p=128)
    outr = ap_out.rearrange("(c p) d -> c p d", p=128)

    dma_sem_names = (
        ["c_ld", "x0", "x1", "ps0", "ps1", "rg0", "rg1", "rg2", "rv",
         "vs0", "vs1", "vs2", "vs3", "w2r", "w2s", "g2l", "fl", "agr", "agi",
         "wo", "oq", "xr", "r2s", "p4w0", "p4w1", "ys0", "ys1", "yl",
         "r50", "r51", "wt0", "wt1", "ob0", "ob1", "ob2", "ob3", "dbg"])
    n = {s: 0 for s in
         ["pe", "dv", "ac", "cc", "bar"] + dma_sem_names}

    with ExitStack() as top:
        sem = {s: top.enter_context(nc.semaphore("s_" + s)) for s in n}
        blk = top.enter_context(nc.Block())

        def I(instr, s, k=1):
            instr.then_inc(sem[s], k)
            n[s] += k
            return n[s]

        def W(eng, s, v):
            assert v <= n[s], f"wait {s}>={v} but only {n[s]} emitted"
            if v > 0:
                eng.wait_ge(sem[s], v)

        def FE(e, s, ins):
            v = I(ins, s, 1)
            e.wait_ge(sem[s], v)

        def dma(eng, q, out, in_):
            return I(eng.dma_start(out=out, in_=in_), q, 16)

        def barrier():
            fins = [(s, n[s]) for s in
                    ["pe", "dv", "ac", "cc"] + dma_sem_names if n[s] > 0]

            def _b(e):
                for s, v in fins:
                    W(e, s, v)
            blk.sync(_b)
            blk.gpsimd(_b)
            blk.tensor(_b)
            blk.vector(_b)
            blk.scalar(_b)

        idb = top.enter_context(nc.sbuf_tensor("idb", [128, 128], BF16))
        h128s = top.enter_context(nc.sbuf_tensor("h128s", [128, 128], BF16))
        h64i2 = top.enter_context(nc.sbuf_tensor("h64i2", [128, 128], BF16))
        maskut = top.enter_context(nc.sbuf_tensor("maskut", [128, 128], F32))
        cmask = top.enter_context(nc.sbuf_tensor("cmask", [128, 8], F32))
        icmask = top.enter_context(nc.sbuf_tensor("icmask", [128, 8], F32))
        zer = top.enter_context(nc.sbuf_tensor("zer", [128, 128], F32))
        sin_bf = top.enter_context(nc.sbuf_tensor("sin_bf", [128, H * DK], BF16))
        stc = top.enter_context(nc.sbuf_tensor("stc", [128, 64], F32))

        @blk.sync
        def _(e):
            for src, dst in ((c_ident, idb), (c_h128s, h128s), (c_h64i2, h64i2),
                             (c_maskut, maskut), (c_cmask, cmask),
                             (c_icmask, icmask)):
                dma(e, "c_ld", dst[:, :], src)

        cvals = [EPS, 1.0, -LOG_SQRT_DK]
        cbuf = top.enter_context(nc.sbuf_tensor("cbuf", [128, len(cvals)], F32))
        for ci_, v_ in enumerate(cvals):
            nc.const_aps.register(F32, v_, cbuf[:, ci_:ci_ + 1])

        @blk.vector
        def _(e):
            e.memset(zer[:, :], 0.0)
            ins = None
            for ci_, v_ in enumerate(cvals):
                ins = e.memset(cbuf[:, ci_:ci_ + 1], v_)
            I(ins, "dv", 1)

        pb = [top.enter_context(nc.psum_tensor(f"pb{i}", [128, 512], F32))
              for i in range(8)]

        with ExitStack() as s1:
            xnf = s1.enter_context(nc.sbuf_tensor("xnf", [128, KD * T], BF16))
            xbuf = [s1.enter_context(nc.sbuf_tensor(f"xb{i}", [128, D], F32))
                    for i in range(2)]
            xnb = [s1.enter_context(nc.sbuf_tensor(f"xnb{i}", [128, D], BF16))
                   for i in range(2)]
            scrA = s1.enter_context(nc.sbuf_tensor("scrA", [128, D], F32))
            scrB = s1.enter_context(nc.sbuf_tensor("scrB", [128, D], F32))

            xload = [0] * NCH
            var_done = [0] * NCH
            rstd_done = [0] * NCH
            xn_done = [0] * NCH
            tmm_done = [[0] * 4 for _ in range(NCH)]
            ev_done = [[0] * 4 for _ in range(NCH)]

            for t in range(NCH):
                @blk.sync
                def _(e, t=t):
                    if t >= 2:
                        W(e, "dv", xn_done[t - 2])
                    xload[t] = dma(e, "x%d" % (t % 2), xbuf[t % 2][:, :], xr[t])

                @blk.vector
                def _(e, t=t):
                    W(e, "x%d" % (t % 2), xload[t])
                    o = (t % 2) * 4
                    e.tensor_scalar(out=scrA[:, :], in0=xbuf[t % 2][:, :],
                                    scalar1=1.0, scalar2=0.0, op0=ALU.mult,
                                    op1=ALU.add, accum_out=stc[:, o:o + 1])
                    ins = e.scalar_tensor_tensor(
                        out=scrB[:, :], in0=xbuf[t % 2][:, :], scalar=1.0,
                        in1=xbuf[t % 2][:, :], op0=ALU.bypass, op1=ALU.mult,
                        accum_out=stc[:, o + 1:o + 2])
                    var_done[t] = I(ins, "dv", 1)

                @blk.scalar
                def _(e, t=t):
                    W(e, "dv", var_done[t])
                    o = (t % 2) * 4
                    FE(e, "ac", e.mul(stc[:, o:o + 1], stc[:, o:o + 1], 1.0 / D))
                    FE(e, "ac", e.activation(stc[:, o + 2:o + 3],
                                             stc[:, o:o + 1], AF.Square))
                    FE(e, "ac", e.mul(stc[:, o + 2:o + 3],
                                      stc[:, o + 2:o + 3], -1.0))
                    FE(e, "ac", e.activation(stc[:, o + 3:o + 4],
                                             stc[:, o + 1:o + 2], AF.Identity,
                                             bias=stc[:, o + 2:o + 3],
                                             scale=1.0 / D))
                    FE(e, "ac", e.activation(stc[:, o + 3:o + 4],
                                             stc[:, o + 3:o + 4], AF.Ln,
                                             bias=EPS))
                    ins = e.activation(stc[:, o + 3:o + 4],
                                       stc[:, o + 3:o + 4], AF.Exp, scale=-0.5)
                    rstd_done[t] = I(ins, "ac", 1)

                @blk.vector
                def _(e, t=t):
                    W(e, "ac", rstd_done[t])
                    if t >= 2:
                        W(e, "pe", tmm_done[t - 2][3])
                    o = (t % 2) * 4
                    ins = e.tensor_scalar(
                        out=xnb[t % 2][:, :], in0=xbuf[t % 2][:, :],
                        scalar1=stc[:, o:o + 1], scalar2=stc[:, o + 3:o + 4],
                        op0=ALU.subtract, op1=ALU.mult)
                    xn_done[t] = I(ins, "dv", 1)

                @blk.tensor
                def _(e, t=t):
                    if t == 0:
                        W(e, "c_ld", 96)
                    W(e, "dv", xn_done[t])
                    for grp in range(4):
                        if t >= 1:
                            W(e, "dv", ev_done[t - 1][grp])
                        ins = None
                        for d4 in range(4):
                            dj = grp * 4 + d4
                            ins = e.matmul(pb[grp][:, d4 * 128:(d4 + 1) * 128],
                                           xnb[t % 2][:, dj * 128:(dj + 1) * 128],
                                           idb[:, :], start=True, stop=True)
                        tmm_done[t][grp] = I(ins, "pe", 1)

                @blk.vector
                def _(e, t=t):
                    for grp in range(4):
                        W(e, "pe", tmm_done[t][grp])
                        dst = xnf[:, :].rearrange("p (j c) -> p j c", j=KD)[
                            :, grp * 4:(grp + 1) * 4, t * 128:(t + 1) * 128]
                        ins = e.tensor_copy(
                            out=dst,
                            in_=pb[grp][:, :].rearrange("p (j c) -> p j c", j=4))
                        ev_done[t][grp] = I(ins, "dv", 1)

            xnf_done = ev_done[NCH - 1][3]

            ring = s1.enter_context(nc.sbuf_tensor("ring", [128, 3 * KD * 128], BF16))
            pstag = [s1.enter_context(nc.sbuf_tensor(f"pst{i}", [128, D], BF16))
                     for i in range(2)]

            ring_guard = [0, 0, 0]
            stag_guard = [0, 0]

            def proj_fm(wap, sp, prev_pair):
                wl = [0] * KD
                mmj = [0] * KD
                evj = [0] * KD
                spj = [0] * KD
                wr = wap.rearrange("(k p) d -> k p d", p=128)
                rv = ring[:, :].rearrange("p (s k c) -> p s k c", s=3, k=KD)
                for j in range(KD):
                    bs = (j % 2) * 4

                    @blk.gpsimd
                    def _(e, j=j):
                        if ring_guard[j % 3]:
                            W(e, "pe", ring_guard[j % 3])
                        wl[j] = dma(e, "rg%d" % (j % 3), rv[:, j % 3],
                                    wr[:, :, j * 128:(j + 1) * 128]
                                    .transpose([1, 0, 2]))

                    @blk.tensor
                    def _(e, j=j, bs=bs):
                        W(e, "rg%d" % (j % 3), wl[j])
                        if j <= 1:
                            if prev_pair[j]:
                                W(e, "dv", prev_pair[j])
                            if j == 0:
                                W(e, "dv", xnf_done)
                        else:
                            W(e, "dv", evj[j - 2])
                        ins = None
                        for g in range(NG):
                            for k in range(KD):
                                ins = e.matmul(
                                    pb[bs + g][:, :],
                                    ring[:, ((j % 3) * KD + k) * 128:
                                         ((j % 3) * KD + k) * 128 + 128],
                                    xnf[:, k * T + g * G:k * T + (g + 1) * G],
                                    start=(k == 0), stop=(k == KD - 1))
                        mmj[j] = I(ins, "pe", 1)
                    ring_guard[j % 3] = mmj[j]

                    @blk.vector
                    def _(e, j=j, bs=bs):
                        W(e, "pe", mmj[j])
                        if stag_guard[j % 2]:
                            W(e, "ps%d" % (j % 2), stag_guard[j % 2])
                        ins = None
                        for g in range(NG):
                            ins = e.tensor_copy(
                                out=pstag[j % 2][:, g * G:(g + 1) * G],
                                in_=pb[bs + g][:, :])
                        evj[j] = I(ins, "dv", 1)

                    @blk.sync
                    def _(e, j=j):
                        W(e, "dv", evj[j])
                        spj[j] = dma(e, "ps%d" % (j % 2), sp[j],
                                     pstag[j % 2][:, :])
                    stag_guard[j % 2] = spj[j]
                return (evj[KD - 2], evj[KD - 1])

            prev_ev = (0, 0)
            for wnm, sp in (("wq", sp_q), ("wk", sp_k), ("wg", sp_g)):
                prev_ev = proj_fm(ap_w[wnm], sp, prev_ev)

            ringv = s1.enter_context(nc.sbuf_tensor("ringv", [128, KD * G], BF16))
            vstag = [s1.enter_context(nc.sbuf_tensor(f"vst{i}", [128, G], BF16))
                     for i in range(4)]
            wvr = ap_w["wv"].rearrange("(k p) d -> k p d", p=128)
            rvv = ringv[:, :].rearrange("p (k c) -> p k c", k=KD)
            vwl = [0] * NG
            vmm = [[0] * NCH for _ in range(NG)]
            vev = [[0] * NCH for _ in range(NG)]
            vsp = [[0] * NCH for _ in range(NG)]
            for dvg in range(NG):
                @blk.gpsimd
                def _(e, dvg=dvg):
                    if dvg >= 1:
                        W(e, "pe", vmm[dvg - 1][NCH - 1])
                    vwl[dvg] = dma(e, "rv", rvv,
                                   wvr[:, :, dvg * G:(dvg + 1) * G]
                                   .transpose([1, 0, 2]))

                for tc in range(NCH):
                    @blk.tensor
                    def _(e, dvg=dvg, tc=tc):
                        if tc == 0:
                            W(e, "rv", vwl[dvg])
                        if dvg == 0 and tc < 4:
                            W(e, "dv", prev_ev[0])
                        else:
                            pv = vev[dvg][tc - 4] if tc >= 4 else \
                                vev[dvg - 1][NCH - 4 + tc]
                            W(e, "dv", pv)
                        ins = None
                        for k in range(KD):
                            ins = e.matmul(
                                pb[tc % 4][:, :],
                                xnf[:, k * T + tc * 128:k * T + (tc + 1) * 128],
                                ringv[:, k * G:(k + 1) * G],
                                start=(k == 0), stop=(k == KD - 1))
                        vmm[dvg][tc] = I(ins, "pe", 1)

                    @blk.vector
                    def _(e, dvg=dvg, tc=tc):
                        W(e, "pe", vmm[dvg][tc])
                        pv = vsp[dvg][tc - 4] if tc >= 4 else \
                            (vsp[dvg - 1][NCH - 4 + tc] if dvg >= 1 else 0)
                        if pv:
                            W(e, "vs%d" % (tc % 4), pv)
                        ins = e.tensor_copy(out=vstag[tc % 4][:, :],
                                            in_=pb[tc % 4][:, :])
                        vev[dvg][tc] = I(ins, "dv", 1)

                    @blk.sync
                    def _(e, dvg=dvg, tc=tc):
                        W(e, "dv", vev[dvg][tc])
                        vsp[dvg][tc] = dma(e, "vs%d" % (tc % 4),
                                           sp_v[tc, :, dvg * G:(dvg + 1) * G],
                                           vstag[tc % 4][:, :])

            w2row = s1.enter_context(nc.sbuf_tensor("w2row", [128, HF], BF16))
            w2st = s1.enter_context(nc.sbuf_tensor("w2st", [128, AH * 128], BF16))
            w2rr = ap_w2.rearrange("(i p) h -> i p h", p=128)
            w2l = [0] * KD
            w2mm = [[0] * KD for _ in range(KD)]
            w2ev = [[0] * KD for _ in range(KD)]
            w2sp = [0] * KD
            for i in range(KD):
                @blk.gpsimd
                def _(e, i=i):
                    if i >= 1:
                        W(e, "pe", w2mm[i - 1][KD - 1])
                    w2l[i] = dma(e, "w2r", w2row[:, :], w2rr[i])

                for half in range(2):
                    @blk.tensor
                    def _(e, i=i, half=half):
                        if half == 0:
                            W(e, "w2r", w2l[i])
                            if i >= 1:
                                W(e, "w2s", w2sp[i - 1])
                        for q4 in range(half * 8, half * 8 + 8):
                            bk = q4 % 8
                            if i == 0 and half == 0:
                                if bk < 4:
                                    W(e, "dv", vev[NG - 1][NCH - 1])
                                elif bk == 4:
                                    W(e, "dv", prev_ev[1])
                            else:
                                pv = w2ev[i][q4 - 8] if q4 >= 8 else \
                                    w2ev[i - 1][KD - 8 + q4]
                                W(e, "dv", pv)
                            ins = None
                            for s4 in range(4):
                                a = q4 * 4 + s4
                                ins = e.matmul(
                                    pb[bk][:, s4 * 128:(s4 + 1) * 128],
                                    w2row[:, a * 128:(a + 1) * 128],
                                    idb[:, :], start=True, stop=True)
                            w2mm[i][q4] = I(ins, "pe", 1)

                    @blk.vector
                    def _(e, i=i, half=half):
                        for q4 in range(half * 8, half * 8 + 8):
                            W(e, "pe", w2mm[i][q4])
                            ins = e.tensor_copy(
                                out=w2st[:, q4 * 512:(q4 + 1) * 512],
                                in_=pb[q4 % 8][:, :])
                            w2ev[i][q4] = I(ins, "dv", 1)

                @blk.sync
                def _(e, i=i):
                    W(e, "dv", w2ev[i][KD - 1])
                    w2sp[i] = dma(
                        e, "w2s",
                        sp_w2t[:, :, i * 128:(i + 1) * 128].transpose([1, 0, 2]),
                        w2st[:, :].rearrange("p (a c) -> p a c", a=AH))

        barrier()

        with ExitStack() as s2:
            Sf = s2.enter_context(nc.sbuf_tensor("Sf", [128, H * DK], F32))
            Sb = s2.enter_context(nc.sbuf_tensor("Sb", [128, H * DK], BF16))
            bpre = s2.enter_context(nc.sbuf_tensor("bpre", [128, H], F32))
            epre = s2.enter_context(nc.sbuf_tensor("epre", [128, H], F32))
            dcol = s2.enter_context(nc.sbuf_tensor("dcol", [128, H], F32))
            bhat = s2.enter_context(nc.sbuf_tensor("bhat", [128, H], F32))
            agbuf = s2.enter_context(nc.sbuf_tensor("agbuf", [128, AGW], F32))
            s2a = s2.enter_context(ExitStack())
            qg = s2a.enter_context(nc.sbuf_tensor("qg", [128, H * G], BF16))
            kg = s2a.enter_context(nc.sbuf_tensor("kg", [128, H * G], BF16))
            gg = s2a.enter_context(nc.sbuf_tensor("gg", [128, H * G], BF16))
            vg = s2a.enter_context(nc.sbuf_tensor("vg", [128, 4 * D], BF16))
            sB = s2a.enter_context(nc.sbuf_tensor("sB", [128, D], F32))
            Braw = s2a.enter_context(nc.sbuf_tensor("Braw", [128, D], F32))
            sc1 = s2a.enter_context(nc.sbuf_tensor("sc1", [128, D], F32))
            sc2 = s2a.enter_context(nc.sbuf_tensor("sc2", [128, D], F32))
            eqb = s2a.enter_context(nc.sbuf_tensor("eqb", [128, D], BF16))
            enb = s2a.enter_context(nc.sbuf_tensor("enb", [128, D], BF16))
            qtb = s2a.enter_context(nc.sbuf_tensor("qtb", [128, D], BF16))
            ktb = s2a.enter_context(nc.sbuf_tensor("ktb", [128, D], BF16))
            khb = s2a.enter_context(nc.sbuf_tensor("khb", [128, D], BF16))
            amb = s2a.enter_context(nc.sbuf_tensor("amb", [128, D], BF16))
            ktm = s2a.enter_context(nc.sbuf_tensor("ktm", [128, D], BF16))
            qdst = s2a.enter_context(nc.sbuf_tensor("qdst", [128, H * G], BF16))
            ost = s2a.enter_context(nc.sbuf_tensor("ost", [128, H * G], F32))

            @blk.vector
            def _(e):
                e.memset(Sf[:, :], 0.0)
                e.memset(Sb[:, :], 0.0)
                ins = e.memset(bpre[:, :], 0.0)
                I(ins, "dv", 1)
            p2_init = n["dv"]

            def gv(sp, g):
                return sp[:, :, g * G:(g + 1) * G].transpose([1, 0, 2])

            gld = [0] * NG
            flush_done = [0] * NG
            ls_done = [0] * NCH
            scan_done = [0] * NCH
            exps_done = [0] * NCH
            fac_done = [0] * NCH
            akt_done = [0] * NCH
            am_done = [0] * NCH
            last_pe = [0] * NCH
            last_dv = [0] * NCH

            Bv = Braw[:, :].rearrange("p (h t) -> p h t", h=H)
            bcv = Bv[:, :, 127:128].squeeze(2)

            for g in range(NG):
                @blk.sync
                def _(e, g=g):
                    if g >= 1:
                        W(e, "dv", last_dv[g * 4 - 1])
                    dma(e, "g2l", qg[:, :].rearrange("p (j t) -> p j t", j=H),
                        gv(sp_q, g))
                    dma(e, "g2l", kg[:, :].rearrange("p (j t) -> p j t", j=H),
                        gv(sp_k, g))
                    dma(e, "g2l", gg[:, :].rearrange("p (j t) -> p j t", j=H),
                        gv(sp_g, g))
                    gld[g] = dma(e, "g2l",
                                 vg[:, :].rearrange("p (c d) -> p c d", c=4),
                                 sp_v[g * 4:(g + 1) * 4].transpose([1, 0, 2]))

                for ci in range(4):
                    t = g * 4 + ci

                    def chs(buf, ci=ci):
                        return buf[:, :].rearrange("p (j t) -> p j t", j=H)[
                            :, :, ci * 128:(ci + 1) * 128]

                    @blk.scalar
                    def _(e, g=g, ci=ci, t=t):
                        W(e, "g2l", gld[g])
                        if t == 0:
                            W(e, "dv", p2_init)
                        else:
                            W(e, "dv", fac_done[t - 1])
                        gsl = chs(gg)
                        v1 = sc1[:, :].rearrange("p (j t) -> p j t", j=H)
                        v2 = sc2[:, :].rearrange("p (j t) -> p j t", j=H)
                        FE(e, "ac", e.activation(v1, gsl, AF.Abs))
                        FE(e, "ac", e.activation(v2, v1, AF.Exp, scale=-1.0))
                        FE(e, "ac", e.activation(v1, v2, AF.Ln, bias=1.0))
                        e.activation(v2, gsl, AF.Relu, scale=-1.0)
                        ins = e.activation(epre[:, :], bpre[:, :], AF.Exp,
                                           scale=-1.0 / 16.0)
                        ls_done[t] = I(ins, "ac", 1)

                    @blk.vector
                    def _(e, g=g, t=t):
                        W(e, "ac", ls_done[t])
                        if t >= 1:
                            W(e, "ac", exps_done[t - 1])
                        if g >= 1 and t % 4 == 0:
                            W(e, "fl", flush_done[g - 1])
                        FE(e, "dv", e.tensor_add(sB[:, :], sc1[:, :],
                                                 sc2[:, :]))
                        ins = None
                        for h in range(H):
                            ins = e.tensor_tensor_scan(
                                out=Braw[:, h * 128:(h + 1) * 128],
                                data0=sB[:, h * 128:(h + 1) * 128],
                                data1=zer[:, :], initial=0.0,
                                op0=ALU.add, op1=ALU.add)
                        scan_fence = I(ins, "dv", 1)
                        e.wait_ge(sem["dv"], scan_fence)
                        ins = e.tensor_add(bpre[:, :], bpre[:, :], bcv)
                        scan_done[t] = I(ins, "dv", 1)

                    @blk.scalar
                    def _(e, t=t):
                        W(e, "dv", scan_done[t])
                        if t >= 1:
                            W(e, "dv", last_dv[t - 1])
                        e.activation(eqb[:, :], Braw[:, :], AF.Exp,
                                     scale=-1.0 / 16.0, bias=-LOG_SQRT_DK)
                        e.activation(enb[:, :], Braw[:, :], AF.Exp,
                                     scale=1.0 / 16.0)
                        ins = e.activation(dcol[:, :], bcv, AF.Exp,
                                           scale=-1.0 / 16.0)
                        exps_done[t] = I(ins, "ac", 1)

                    @blk.vector
                    def _(e, g=g, ci=ci, t=t):
                        W(e, "ac", exps_done[t])
                        if t >= 1:
                            W(e, "pe", last_pe[t - 1])
                        ev = lambda b: b[:, :].rearrange("p (j t) -> p j t", j=H)
                        e.tensor_mul(ev(qtb), chs(qg), ev(eqb))
                        FE(e, "dv", e.tensor_mul(ev(ktb), chs(kg), ev(enb)))
                        ins = None
                        for h in range(H):
                            ins = e.tensor_scalar_mul(
                                out=khb[:, h * 128:(h + 1) * 128],
                                in0=ktb[:, h * 128:(h + 1) * 128],
                                scalar1=dcol[:, h:h + 1])
                        FE(e, "dv", ins)
                        ins = None
                        for h in range(H):
                            ins = e.tensor_scalar_mul(
                                out=qdst[:, h * G + ci * 128:
                                         h * G + (ci + 1) * 128],
                                in0=qtb[:, h * 128:(h + 1) * 128],
                                scalar1=epre[:, h:h + 1])
                        fac_done[t] = I(ins, "dv", 1)

                    @blk.tensor
                    def _(e, t=t):
                        W(e, "dv", fac_done[t])
                        ins = None
                        for h in range(H):
                            ins = e.matmul(
                                pb[h // 4][:, (h % 4) * 128:(h % 4 + 1) * 128],
                                ktb[:, h * 128:(h + 1) * 128],
                                qtb[:, h * 128:(h + 1) * 128],
                                start=True, stop=True)
                        for h in range(H):
                            ins = e.matmul(
                                pb[4 + h // 4][:, (h % 4) * 128:(h % 4 + 1) * 128],
                                khb[:, h * 128:(h + 1) * 128],
                                idb[:, :], start=True, stop=True)
                        akt_done[t] = I(ins, "pe", 1)

                    @blk.vector
                    def _(e, t=t):
                        W(e, "pe", akt_done[t])
                        if t == 0:
                            W(e, "c_ld", 96)
                        for h in range(H):
                            e.tensor_tensor(
                                out=amb[:, h * 128:(h + 1) * 128],
                                in0=pb[h // 4][:, (h % 4) * 128:(h % 4 + 1) * 128],
                                in1=maskut[:, :], op=ALU.mult)
                        ins = None
                        for h in range(H):
                            ins = e.tensor_copy(
                                out=ktm[:, h * 128:(h + 1) * 128],
                                in_=pb[4 + h // 4][:, (h % 4) * 128:
                                                   (h % 4 + 1) * 128])
                        am_done[t] = I(ins, "dv", 1)

                    @blk.tensor
                    def _(e, ci=ci, t=t):
                        W(e, "dv", am_done[t])
                        ins = None
                        for h in range(H):
                            e.matmul(
                                pb[h // 4][:, (h % 4) * 128:(h % 4 + 1) * 128],
                                Sb[:, h * 128:(h + 1) * 128],
                                qtb[:, h * 128:(h + 1) * 128],
                                start=True, stop=False)
                            ins = e.matmul(
                                pb[h // 4][:, (h % 4) * 128:(h % 4 + 1) * 128],
                                vg[:, ci * D + h * 128:ci * D + (h + 1) * 128],
                                amb[:, h * 128:(h + 1) * 128],
                                start=False, stop=True)
                        I(ins, "pe", 1)
                        for h in range(H):
                            ins = e.matmul(
                                pb[4 + h // 4][:, (h % 4) * 128:(h % 4 + 1) * 128],
                                ktm[:, h * 128:(h + 1) * 128],
                                vg[:, ci * D + h * 128:ci * D + (h + 1) * 128],
                                start=True, stop=True)
                        last_pe[t] = I(ins, "pe", 1)

                    @blk.vector
                    def _(e, ci=ci, t=t):
                        W(e, "pe", last_pe[t])
                        for h in range(H):
                            e.tensor_copy(
                                out=ost[:, h * G + ci * 128:h * G + (ci + 1) * 128],
                                in_=pb[h // 4][:, (h % 4) * 128:(h % 4 + 1) * 128])
                        for h in range(H):
                            ins = e.scalar_tensor_tensor(
                                out=Sf[:, h * 128:(h + 1) * 128],
                                in0=Sf[:, h * 128:(h + 1) * 128],
                                scalar=dcol[:, h:h + 1],
                                in1=pb[4 + h // 4][:, (h % 4) * 128:
                                                   (h % 4 + 1) * 128],
                                op0=ALU.mult, op1=ALU.add)
                        FE(e, "dv", ins)
                        ins = e.tensor_copy(out=Sb[:, :], in_=Sf[:, :])
                        last_dv[t] = I(ins, "dv", 1)

                if dump:
                    for ci in range(4):
                        @blk.sync
                        def _(e, g=g, ci=ci):
                            W(e, "dv", last_dv[g * 4 + ci])
                            dma(e, "dbg", sp_sdbg[g * 4 + ci], Sf[:, :])
                            if g == 0 and ci == 0:
                                dma(e, "dbg", sp_kdbg[0], ktm[:, :].bitcast(F32))
                                dma(e, "dbg", sp_kdbg[1], khb[:, :].bitcast(F32))

                @blk.sync
                def _(e, g=g):
                    W(e, "dv", last_dv[g * 4 + 3])
                    dma(e, "fl", gv(sp_qd, g),
                        qdst[:, :].rearrange("p (j t) -> p j t", j=H))
                    flush_done[g] = dma(e, "fl", gv(sp_o, g),
                                        ost[:, :].rearrange("p (j t) -> p j t", j=H))

            s2a.close()

            @blk.scalar
            def _(e):
                W(e, "dv", last_dv[NCH - 1])
                ins = e.activation(agbuf[:, H * DK:], bpre[:, :], AF.Exp,
                                   scale=-1.0 / 16.0)
                I(ins, "ac", 1)
            dtot_ac = n["ac"]

            @blk.vector
            def _(e):
                W(e, "ac", dtot_ac)
                ins = e.tensor_copy(out=agbuf[:, :H * DK], in_=Sf[:, :])
                I(ins, "dv", 1)
            agpack_dv = n["dv"]

            @blk.gpsimd
            def _(e):
                W(e, "dv", agpack_dv)
                v_agi = dma(e, "agi", ag_in.ap(), agbuf[:, :])
                W(e, "agi", v_agi)
                e.collective_compute(
                    "AllGather", ALU.bypass,
                    replica_groups=[list(range(NCORE))],
                    ins=[ag_in.ap().opt()],
                    outs=[ag_out.ap().opt()],
                ).then_inc(sem["cc"], 1)
                n["cc"] += 1
            cc_done = n["cc"]

            with ExitStack() as s2b:
                agr = s2b.enter_context(nc.sbuf_tensor("agr", [128, 7 * AGW], F32))
                tmsk = s2b.enter_context(nc.sbuf_tensor("tmsk", [128, H * DK], F32))
                sinf = s2b.enter_context(nc.sbuf_tensor("sinf", [128, H * DK], F32))
                dhat = s2b.enter_context(nc.sbuf_tensor("dhat", [128, H], F32))

                @blk.sync
                def _(e):
                    W(e, "cc", cc_done)
                    W(e, "fl", flush_done[NG - 1])
                    agv = ag_out.ap().rearrange("(r p) w -> r p w", p=128)
                    dma(e, "agr", agr[:, :].rearrange("p (r w) -> p r w", r=7),
                        agv[0:7].transpose([1, 0, 2]))
                agr_ld = n["agr"]

                @blk.vector
                def _(e):
                    W(e, "agr", agr_ld)
                    e.memset(sinf[:, :], 0.0)
                    for b in range(7):
                        base = b * AGW
                        e.tensor_scalar(out=dhat[:, :],
                                        in0=agr[:, base + H * DK:base + AGW],
                                        scalar1=cmask[:, b:b + 1],
                                        scalar2=icmask[:, b:b + 1],
                                        op0=ALU.mult, op1=ALU.add)
                        FE(e, "dv", e.tensor_scalar_mul(
                            out=tmsk[:, :],
                            in0=agr[:, base:base + H * DK],
                            scalar1=cmask[:, b:b + 1]))
                        for h in range(H):
                            ins = e.scalar_tensor_tensor(
                                out=sinf[:, h * 128:(h + 1) * 128],
                                in0=sinf[:, h * 128:(h + 1) * 128],
                                scalar=dhat[:, h:h + 1],
                                in1=tmsk[:, h * 128:(h + 1) * 128],
                                op0=ALU.mult, op1=ALU.add)
                        FE(e, "dv", ins)
                    ins = e.tensor_copy(out=sin_bf[:, :], in_=sinf[:, :])
                    I(ins, "dv", 1)
        sin_done = n["dv"]

        barrier()

        with ExitStack() as s3:
            xnf2 = s3.enter_context(nc.sbuf_tensor("xnf2", [128, KD * T], BF16))

            with ExitStack() as s3a:
                wob = s3a.enter_context(nc.sbuf_tensor("wob", [128, KD * D], BF16))
                obuf = s3a.enter_context(nc.sbuf_tensor("obuf", [128, 4 * G], F32))
                qdb = s3a.enter_context(nc.sbuf_tensor("qdb", [128, 4 * G], BF16))
                ofull = s3a.enter_context(nc.sbuf_tensor("ofull", [128, H * G], BF16))
                abuf = s3a.enter_context(nc.sbuf_tensor("abuf", [128, D], F32))
                sqd = s3a.enter_context(nc.sbuf_tensor("sqd", [128, 1024], F32))
                xre = s3a.enter_context(nc.sbuf_tensor("xre", [128, D], F32))
                r2t = s3a.enter_context(nc.sbuf_tensor("r2t", [128, D], F32))
                scr3 = s3a.enter_context(nc.sbuf_tensor("scr3", [128, D], F32))
                xnb3 = s3a.enter_context(nc.sbuf_tensor("xnb3", [128, D], BF16))

                wor = ap_w["wo"].rearrange("(k p) d -> k p d", p=128)

                @blk.gpsimd
                def _(e):
                    for kk in range(KD):
                        dma(e, "wo", wob[:, kk * D:(kk + 1) * D], wor[kk])
                wo_ld = n["wo"]

                corr_pe = [0] * 8
                corr_add_dv = [0] * 4
                corr_dv_last = 0
                wo_last_pe = 0
                wo_pe = [0] * 4
                wo_ev = [0] * 4
                ln3s_prev = 0
                r2sp_prev = 0
                xn2t_pe = [0] * NCH
                xn2g_pe = [0] * 4
                xn2ev_dv = [[0] * 4 for _ in range(NCH)]
                xn2ev_last = 0
                half_ld_box = [0]
                xre_ld_box = [0]
                r2sp_box = [0]

                for g in range(NG):
                    for hb in range(4):
                        @blk.sync
                        def _(e, g=g, hb=hb):
                            if corr_dv_last:
                                W(e, "dv", corr_dv_last)
                            dma(e, "oq",
                                obuf[:, :].rearrange("p (j t) -> p j t", j=4),
                                sp_o[hb * 4:(hb + 1) * 4, :, g * G:(g + 1) * G]
                                .transpose([1, 0, 2]))
                            half_ld_box[0] = dma(
                                e, "oq",
                                qdb[:, :].rearrange("p (j t) -> p j t", j=4),
                                sp_qd[hb * 4:(hb + 1) * 4, :, g * G:(g + 1) * G]
                                .transpose([1, 0, 2]))

                        @blk.tensor
                        def _(e, g=g, hb=hb):
                            W(e, "oq", half_ld_box[0])
                            if g == 0 and hb == 0:
                                W(e, "dv", sin_done)
                            if g >= 1 and hb == 0:
                                W(e, "dv", xn2ev_last)
                            for hh in range(4):
                                h = hb * 4 + hh
                                if corr_add_dv[hh]:
                                    W(e, "dv", corr_add_dv[hh])
                                ins = e.matmul(pb[hh][:, :],
                                               sin_bf[:, h * 128:(h + 1) * 128],
                                               qdb[:, hh * G:(hh + 1) * G],
                                               start=True, stop=True)
                                corr_pe[hh] = I(ins, "pe", 1)

                        @blk.vector
                        def _(e, g=g, hb=hb):
                            if wo_last_pe and hb == 0:
                                W(e, "pe", wo_last_pe)
                            for hh in range(4):
                                h = hb * 4 + hh
                                W(e, "pe", corr_pe[hh])
                                ins = e.tensor_tensor(
                                    out=ofull[:, h * G:(h + 1) * G],
                                    in0=pb[hh][:, :],
                                    in1=obuf[:, hh * G:(hh + 1) * G],
                                    op=ALU.add)
                                corr_add_dv[hh] = I(ins, "dv", 1)
                        corr_dv_last = n["dv"]

                    corr_done_g = corr_dv_last

                    for tc4 in range(4):
                        t = g * 4 + tc4

                        @blk.sync
                        def _(e, t=t):
                            if ln3s_prev:
                                W(e, "dv", ln3s_prev)
                            xre_ld_box[0] = dma(e, "xr", xre[:, :], xr[t])

                        @blk.tensor
                        def _(e, g=g, tc4=tc4):
                            W(e, "dv", corr_done_g)
                            W(e, "wo", wo_ld)
                            for dg in range(4):
                                if wo_ev[dg]:
                                    W(e, "ac", wo_ev[dg])
                                ins = None
                                for kk in range(KD):
                                    ins = e.matmul(
                                        pb[4 + dg][:, :],
                                        ofull[:, kk * G + tc4 * 128:
                                              kk * G + tc4 * 128 + 128],
                                        wob[:, kk * D + dg * 512:
                                            kk * D + (dg + 1) * 512],
                                        start=(kk == 0), stop=(kk == KD - 1))
                                wo_pe[dg] = I(ins, "pe", 1)
                        wo_last_pe = n["pe"]

                        @blk.scalar
                        def _(e):
                            if ln3s_prev:
                                W(e, "dv", ln3s_prev)
                            for dg in range(4):
                                W(e, "pe", wo_pe[dg])
                                I(e.activation(abuf[:, dg * 512:(dg + 1) * 512],
                                               pb[4 + dg][:, :], AF.Copy,
                                               accum_out=stc[:, 16 + dg:17 + dg]),
                                  "ac", 1)
                                ins = e.activation(
                                    sqd[:, (dg % 2) * 512:(dg % 2) * 512 + 512],
                                    pb[4 + dg][:, :], AF.Square,
                                    accum_out=stc[:, 20 + dg:21 + dg])
                                wo_ev[dg] = I(ins, "ac", 1)
                                e.wait_ge(sem["ac"], wo_ev[dg])

                        @blk.vector
                        def _(e):
                            W(e, "ac", wo_ev[3])
                            e.reduce_sum(stc[:, 24:25], stc[:, 16:20], axis=AXX)
                            ins = e.reduce_sum(stc[:, 25:26], stc[:, 20:24],
                                               axis=AXX)
                            I(ins, "dv", 1)
                        ln2s_dv = n["dv"]

                        @blk.scalar
                        def _(e):
                            W(e, "dv", ln2s_dv)
                            FE(e, "ac", e.mul(stc[:, 24:25], stc[:, 24:25],
                                              1.0 / D))
                            FE(e, "ac", e.activation(stc[:, 26:27],
                                                     stc[:, 24:25], AF.Square))
                            FE(e, "ac", e.mul(stc[:, 26:27], stc[:, 26:27],
                                              -1.0))
                            FE(e, "ac", e.activation(stc[:, 27:28],
                                                     stc[:, 25:26], AF.Identity,
                                                     bias=stc[:, 26:27],
                                                     scale=1.0 / D))
                            FE(e, "ac", e.activation(stc[:, 27:28],
                                                     stc[:, 27:28], AF.Ln,
                                                     bias=EPS))
                            ins = e.activation(stc[:, 27:28], stc[:, 27:28],
                                               AF.Exp, scale=-0.5)
                            I(ins, "ac", 1)
                        ln2r_ac = n["ac"]

                        @blk.vector
                        def _(e):
                            W(e, "ac", ln2r_ac)
                            W(e, "xr", xre_ld_box[0])
                            if r2sp_prev:
                                W(e, "r2s", r2sp_prev)
                            FE(e, "dv", e.tensor_scalar(
                                out=scr3[:, :], in0=abuf[:, :],
                                scalar1=stc[:, 24:25], scalar2=stc[:, 27:28],
                                op0=ALU.subtract, op1=ALU.mult))
                            FE(e, "dv", e.tensor_add(r2t[:, :], scr3[:, :],
                                                     xre[:, :]))
                            FE(e, "dv", e.tensor_scalar(
                                out=scr3[:, :], in0=r2t[:, :],
                                scalar1=1.0, scalar2=0.0,
                                op0=ALU.mult, op1=ALU.add,
                                accum_out=stc[:, 28:29]))
                            ins = e.scalar_tensor_tensor(
                                out=scr3[:, :], in0=r2t[:, :], scalar=1.0,
                                in1=r2t[:, :], op0=ALU.bypass, op1=ALU.mult,
                                accum_out=stc[:, 29:30])
                            I(ins, "dv", 1)
                        ln3s_dv = n["dv"]
                        ln3s_prev = ln3s_dv

                        @blk.sync
                        def _(e, t=t):
                            W(e, "dv", ln3s_dv)
                            r2sp_box[0] = dma(e, "r2s", sp_r2[t], r2t[:, :])
                        r2sp_prev = r2sp_box[0]

                        @blk.scalar
                        def _(e):
                            W(e, "dv", ln3s_dv)
                            FE(e, "ac", e.mul(stc[:, 28:29], stc[:, 28:29],
                                              1.0 / D))
                            FE(e, "ac", e.activation(stc[:, 30:31],
                                                     stc[:, 28:29], AF.Square))
                            FE(e, "ac", e.mul(stc[:, 30:31], stc[:, 30:31],
                                              -1.0))
                            FE(e, "ac", e.activation(stc[:, 31:32],
                                                     stc[:, 29:30], AF.Identity,
                                                     bias=stc[:, 30:31],
                                                     scale=1.0 / D))
                            FE(e, "ac", e.activation(stc[:, 31:32],
                                                     stc[:, 31:32], AF.Ln,
                                                     bias=EPS))
                            ins = e.activation(stc[:, 31:32], stc[:, 31:32],
                                               AF.Exp, scale=-0.5)
                            I(ins, "ac", 1)
                        ln3r_ac = n["ac"]

                        @blk.vector
                        def _(e, t=t):
                            W(e, "ac", ln3r_ac)
                            if t >= 1:
                                W(e, "pe", xn2t_pe[t - 1])
                            ins = e.tensor_scalar(
                                out=xnb3[:, :], in0=r2t[:, :],
                                scalar1=stc[:, 28:29], scalar2=stc[:, 31:32],
                                op0=ALU.subtract, op1=ALU.mult)
                            I(ins, "dv", 1)
                        xn3_dv = n["dv"]

                        @blk.tensor
                        def _(e, t=t):
                            W(e, "dv", xn3_dv)
                            for grp in range(4):
                                if t >= 1:
                                    W(e, "dv", xn2ev_dv[t - 1][grp])
                                ins = None
                                for d4 in range(4):
                                    dj = grp * 4 + d4
                                    ins = e.matmul(
                                        pb[grp][:, d4 * 128:(d4 + 1) * 128],
                                        xnb3[:, dj * 128:(dj + 1) * 128],
                                        idb[:, :], start=True, stop=True)
                                xn2g_pe[grp] = I(ins, "pe", 1)
                            xn2t_pe[t] = n["pe"]

                        @blk.vector
                        def _(e, t=t):
                            for grp in range(4):
                                W(e, "pe", xn2g_pe[grp])
                                dst = xnf2[:, :].rearrange(
                                    "p (j c) -> p j c", j=KD)[
                                    :, grp * 4:(grp + 1) * 4,
                                    t * 128:(t + 1) * 128]
                                ins = e.tensor_copy(
                                    out=dst,
                                    in_=pb[grp][:, :].rearrange(
                                        "p (j c) -> p j c", j=4))
                                xn2ev_dv[t][grp] = I(ins, "dv", 1)
                    xn2ev_last = xn2ev_dv[g * 4 + 3][3]
                xnf2_done = xn2ev_dv[NCH - 1][3]

            barrier()

            with ExitStack() as s4:
                w1row = [s4.enter_context(nc.sbuf_tensor(f"w1r{i}", [128, D], BF16))
                         for i in range(2)]
                w3row = [s4.enter_context(nc.sbuf_tensor(f"w3r{i}", [128, D], BF16))
                         for i in range(2)]
                w1T = [s4.enter_context(
                    nc.sbuf_tensor(f"w1T{i}", [128, KD * 128], BF16))
                    for i in range(2)]
                w3T = [s4.enter_context(
                    nc.sbuf_tensor(f"w3T{i}", [128, KD * 128], BF16))
                    for i in range(2)]
                silus = s4.enter_context(nc.sbuf_tensor("silus", [128, G], F32))
                yst = [s4.enter_context(nc.sbuf_tensor(f"yst{i}", [128, T], BF16))
                       for i in range(2)]
                w1r = ap_w1.rearrange("(j p) d -> j p d", p=128)
                w3r = ap_w3.rearrange("(j p) d -> j p d", p=128)

                wrow_ld = [0] * AH
                wT_pe = [0] * AH
                wT_dv = [0] * AH
                up_pe = [[0] * 8 for _ in range(AH)]
                sil_ac = [0] * NG
                y_dv = [[0] * NG for _ in range(AH)]
                ysp_sq = [0] * AH

                for j in range(AH):
                    @blk.gpsimd
                    def _(e, j=j):
                        if j >= 2:
                            W(e, "pe", wT_pe[j - 2])
                        dma(e, "p4w%d" % (j % 2), w1row[j % 2][:, :], w1r[j])
                        wrow_ld[j] = dma(e, "p4w%d" % (j % 2),
                                         w3row[j % 2][:, :], w3r[j])

                    @blk.tensor
                    def _(e, j=j):
                        W(e, "p4w%d" % (j % 2), wrow_ld[j])
                        if j == 0:
                            pass
                        else:
                            W(e, "dv", wT_dv[j - 1])
                            W(e, "dv", y_dv[j - 1][NG - 1])
                        ins = None
                        for i in range(KD):
                            ins = e.matmul(
                                pb[i // 4][:, (i % 4) * 128:(i % 4 + 1) * 128],
                                w1row[j % 2][:, i * 128:(i + 1) * 128],
                                idb[:, :], start=True, stop=True)
                        for i in range(KD):
                            ins = e.matmul(
                                pb[4 + i // 4][:, (i % 4) * 128:(i % 4 + 1) * 128],
                                w3row[j % 2][:, i * 128:(i + 1) * 128],
                                idb[:, :], start=True, stop=True)
                        wT_pe[j] = I(ins, "pe", 1)

                    @blk.vector
                    def _(e, j=j):
                        W(e, "pe", wT_pe[j])
                        ins = None
                        for b in range(4):
                            e.tensor_copy(out=w1T[j % 2][:, b * 512:(b + 1) * 512],
                                          in_=pb[b][:, :])
                            ins = e.tensor_copy(
                                out=w3T[j % 2][:, b * 512:(b + 1) * 512],
                                in_=pb[4 + b][:, :])
                        wT_dv[j] = I(ins, "dv", 1)

                    for g in range(NG):
                        @blk.tensor
                        def _(e, j=j, g=g):
                            if g == 0:
                                W(e, "dv", wT_dv[j])
                            if j >= 1:
                                W(e, "dv", y_dv[j - 1][g])
                            ins = None
                            for kk in range(KD):
                                ins = e.matmul(
                                    pb[g][:, :],
                                    w1T[j % 2][:, kk * 128:(kk + 1) * 128],
                                    xnf2[:, kk * T + g * G:kk * T + (g + 1) * G],
                                    start=(kk == 0), stop=(kk == KD - 1))
                            up_pe[j][g * 2] = I(ins, "pe", 1)
                            for kk in range(KD):
                                ins = e.matmul(
                                    pb[4 + g][:, :],
                                    w3T[j % 2][:, kk * 128:(kk + 1) * 128],
                                    xnf2[:, kk * T + g * G:kk * T + (g + 1) * G],
                                    start=(kk == 0), stop=(kk == KD - 1))
                            up_pe[j][g * 2 + 1] = I(ins, "pe", 1)

                        @blk.scalar
                        def _(e, j=j, g=g):
                            W(e, "pe", up_pe[j][g * 2])
                            if j >= 1 or g >= 1:
                                pj, pg = (j, g - 1) if g >= 1 else (j - 1, NG - 1)
                                W(e, "dv", y_dv[pj][pg])
                            ins = e.activation(silus[:, :], pb[g][:, :], AF.Silu)
                            sil_ac[g] = I(ins, "ac", 1)

                        @blk.vector
                        def _(e, j=j, g=g):
                            W(e, "ac", sil_ac[g])
                            W(e, "pe", up_pe[j][g * 2 + 1])
                            if j >= 2 and g == 0:
                                W(e, "ys%d" % (j % 2), ysp_sq[j - 2])
                            ins = e.scalar_tensor_tensor(
                                out=yst[j % 2][:, g * G:(g + 1) * G],
                                in0=silus[:, :], scalar=1.0, in1=pb[4 + g][:, :],
                                op0=ALU.bypass, op1=ALU.mult)
                            y_dv[j][g] = I(ins, "dv", 1)

                    @blk.sync
                    def _(e, j=j):
                        W(e, "dv", y_dv[j][NG - 1])
                        ysp_sq[j] = dma(e, "ys%d" % (j % 2), sp_h[j],
                                        yst[j % 2][:, :])

        barrier()

        with ExitStack() as s5:
            Yb = s5.enter_context(nc.sbuf_tensor("Yb", [128, AH * G], BF16))
            Zb = s5.enter_context(nc.sbuf_tensor("Zb", [128, (G // 2) * 128], BF16))
            Asb = [s5.enter_context(nc.sbuf_tensor(f"As{i}", [128, 512], BF16))
                   for i in range(2)]
            Tsb = [s5.enter_context(nc.sbuf_tensor(f"Ts{i}", [128, 512], BF16))
                   for i in range(2)]
            w2rg = s5.enter_context(nc.sbuf_tensor("w2rg", [128, 16 * 512], BF16))
            scr5 = s5.enter_context(nc.sbuf_tensor("scr5", [128, 1024], F32))
            fbuf = [s5.enter_context(nc.sbuf_tensor(f"fb{i}", [128, D], F32))
                    for i in range(4)]
            r2re = [s5.enter_context(nc.sbuf_tensor(f"r2r{i}", [128, D], F32))
                    for i in range(2)]

            yld = [0] * NG
            r2ld = [[0] * 4 for _ in range(NG)]
            out_sq = [[0] * 4 for _ in range(NG)]
            out_dv = [[0] * 4 for _ in range(NG)]
            f_ac = [[0] * 4 for _ in range(4)]
            w2mm_g = [0] * 32
            w2_last_pe = 0
            Yv = Yb[:, :].rearrange("p (a t) -> p a t", a=AH)

            mm1_last_prev = [0]
            for g5 in range(NG):
                @blk.sync
                def _(e, g5=g5):
                    if g5 >= 1:
                        W(e, "pe", mm1_last_prev[0])
                    yld[g5] = dma(e, "yl",
                                  Yv, sp_h[:, :, g5 * G:(g5 + 1) * G]
                                  .transpose([1, 0, 2]))

                NQ = G // 8
                mm1_pe = [0] * NQ
                mmT_pe = [0] * NQ
                mm2_pe = [0] * NQ
                evA_dv = [0] * NQ
                evT_dv = [0] * NQ
                evB_dv = [0] * NQ

                for q in range(NQ + 2):
                    @blk.tensor
                    def _(e, g5=g5, q=q):
                        ins = None
                        if 2 <= q < NQ + 2:
                            W(e, "dv", evT_dv[q - 2])
                            for d4 in range(4):
                                ins = e.matmul(
                                    pb[2 + (q - 2) % 2][:, d4 * 128:(d4 + 1) * 128],
                                    Tsb[(q - 2) % 2][:, d4 * 128:(d4 + 1) * 128],
                                    h64i2[:, :], start=True, stop=True)
                            mm2_pe[q - 2] = I(ins, "pe", 1)
                        if 1 <= q < NQ + 1:
                            W(e, "dv", evA_dv[q - 1])
                            if q >= 3:
                                W(e, "dv", evB_dv[q - 3])
                            for d4 in range(4):
                                ins = e.matmul(
                                    pb[2 + (q - 1) % 2][:, d4 * 128:(d4 + 1) * 128],
                                    Asb[(q - 1) % 2][:, d4 * 128:(d4 + 1) * 128],
                                    idb[:, :], start=True, stop=True)
                            mmT_pe[q - 1] = I(ins, "pe", 1)
                        if q < NQ:
                            if q == 0:
                                W(e, "yl", yld[g5])
                                if g5 == 0:
                                    W(e, "c_ld", 96)
                            if q >= 2:
                                W(e, "dv", evA_dv[q - 2])
                            ins = e.matmul(
                                pb[q % 2][:, :], h128s[:, :],
                                Yv[:, :, 8 * q:8 * q + 8],
                                start=True, stop=True)
                            mm1_pe[q] = I(ins, "pe", 1)

                    @blk.vector
                    def _(e, g5=g5, q=q):
                        if 2 <= q < NQ + 2:
                            W(e, "pe", mm2_pe[q - 2])
                            if q == 2 and w2_last_pe:
                                W(e, "pe", w2_last_pe)
                            qq = q - 2
                            src_ap = pb[2 + qq % 2][:, :].rearrange(
                                "p (d c) -> p d c", d=4)
                            dst_ap = Zb[:, :].rearrange(
                                "p (a t) -> p a t", a=AH)[
                                :, :, 8 * qq:8 * qq + 8].rearrange(
                                "p a (d t2) -> p d a t2", d=4)
                            ins = e.tensor_copy(out=dst_ap, in_=src_ap.rearrange(
                                "p d (a t2) -> p d a t2", a=AH))
                            evB_dv[qq] = I(ins, "dv", 1)
                        if 1 <= q < NQ + 1:
                            W(e, "pe", mmT_pe[q - 1])
                            ins = e.tensor_copy(out=Tsb[(q - 1) % 2][:, :],
                                                in_=pb[2 + (q - 1) % 2][:, :])
                            evT_dv[q - 1] = I(ins, "dv", 1)
                        if q < NQ:
                            W(e, "pe", mm1_pe[q])
                            src_ap = pb[q % 2][:, :].rearrange(
                                "p (a d t2) -> p a d t2", a=AH, d=4)
                            dst_ap = Asb[q % 2][:, :].rearrange(
                                "p (d a t2) -> p a d t2", a=AH, d=4)
                            ins = e.tensor_copy(out=dst_ap, in_=src_ap)
                            evA_dv[q] = I(ins, "dv", 1)

                sand_dv = evB_dv[NQ - 1]
                mm1_last_prev[0] = mm1_pe[NQ - 1]

                w2ld = [0] * 32
                for dg in range(4):
                    for ab in range(8):
                        gi = dg * 8 + ab

                        @blk.sync
                        def _(e, g5=g5, dg=dg, ab=ab, gi=gi):
                            if gi >= 2:
                                W(e, "pe", w2mm_g[gi - 2])
                            elif g5 >= 1:
                                W(e, "pe", w2_last_pe)
                            w2ld[gi] = dma(
                                e, "wt%d" % (gi % 2),
                                w2rg[:, (gi % 2) * 4096:(gi % 2) * 4096 + 4096]
                                .rearrange("p (a c) -> p a c", a=8),
                                sp_w2t[ab * 8:(ab + 1) * 8, :,
                                       dg * 512:(dg + 1) * 512]
                                .transpose([1, 0, 2]))

                        @blk.tensor
                        def _(e, g5=g5, dg=dg, ab=ab, gi=gi):
                            W(e, "wt%d" % (gi % 2), w2ld[gi])
                            if ab == 0:
                                W(e, "dv", sand_dv)
                                if dg >= 1:
                                    W(e, "ac", f_ac[dg - 1][3])
                                elif g5 >= 1:
                                    W(e, "ac", f_ac[3][3])
                            ins = None
                            for a8 in range(8):
                                a2 = ab * 8 + a8
                                for tc in range(4):
                                    ins = e.matmul(
                                        pb[4 + tc][:, :],
                                        Zb[:, a2 * G + tc * 128:
                                           a2 * G + (tc + 1) * 128],
                                        w2rg[:, (gi % 2) * 4096 + a8 * 512:
                                             (gi % 2) * 4096 + (a8 + 1) * 512],
                                        start=(a2 == 0), stop=(a2 == AH - 1))
                            w2mm_g[gi] = I(ins, "pe", 1)
                    w2_last_pe = w2mm_g[dg * 8 + 7]

                    @blk.scalar
                    def _(e, g5=g5, dg=dg):
                        W(e, "pe", w2mm_g[dg * 8 + 7])
                        for tc in range(4):
                            if g5 >= 1 and dg == 0:
                                W(e, "ob%d" % tc, out_sq[g5 - 1][tc])
                            I(e.activation(fbuf[tc][:, dg * 512:(dg + 1) * 512],
                                           pb[4 + tc][:, :], AF.Copy,
                                           accum_out=stc[:, 16 + 4 * tc + dg:
                                                         17 + 4 * tc + dg]),
                              "ac", 1)
                            ins = e.activation(
                                scr5[:, (tc % 2) * 512:(tc % 2) * 512 + 512],
                                pb[4 + tc][:, :], AF.Square,
                                accum_out=stc[:, 4 * tc + dg:4 * tc + dg + 1])
                            f_ac[dg][tc] = I(ins, "ac", 1)
                            e.wait_ge(sem["ac"], f_ac[dg][tc])

                for tc in range(4):
                    t = g5 * 4 + tc

                    @blk.sync
                    def _(e, g5=g5, tc=tc, t=t):
                        gp = g5 * 4 + tc - 2
                        if gp >= 0:
                            W(e, "dv", out_dv[gp // 4][gp % 4])
                        r2ld[g5][tc] = dma(e, "r5%d" % (tc % 2),
                                           r2re[tc % 2][:, :], sp_r2[t])

                    @blk.vector
                    def _(e, tc=tc):
                        W(e, "ac", f_ac[3][tc])
                        e.reduce_sum(stc[:, 48:49], stc[:, 16 + 4 * tc:
                                                        20 + 4 * tc], axis=AXX)
                        ins = e.reduce_sum(stc[:, 49:50],
                                           stc[:, 4 * tc:4 * tc + 4], axis=AXX)
                        I(ins, "dv", 1)
                    l4s_dv = n["dv"]

                    @blk.scalar
                    def _(e):
                        W(e, "dv", l4s_dv)
                        FE(e, "ac", e.mul(stc[:, 48:49], stc[:, 48:49],
                                          1.0 / D))
                        FE(e, "ac", e.activation(stc[:, 50:51], stc[:, 48:49],
                                                 AF.Square))
                        FE(e, "ac", e.mul(stc[:, 50:51], stc[:, 50:51], -1.0))
                        FE(e, "ac", e.activation(stc[:, 51:52], stc[:, 49:50],
                                                 AF.Identity,
                                                 bias=stc[:, 50:51],
                                                 scale=1.0 / D))
                        FE(e, "ac", e.activation(stc[:, 51:52], stc[:, 51:52],
                                                 AF.Ln, bias=EPS))
                        ins = e.activation(stc[:, 51:52], stc[:, 51:52],
                                           AF.Exp, scale=-0.5)
                        I(ins, "ac", 1)
                    l4r_ac = n["ac"]

                    @blk.vector
                    def _(e, g5=g5, tc=tc):
                        W(e, "ac", l4r_ac)
                        W(e, "r5%d" % (tc % 2), r2ld[g5][tc])
                        FE(e, "dv", e.tensor_scalar(
                            out=fbuf[tc][:, :], in0=fbuf[tc][:, :],
                            scalar1=stc[:, 48:49], scalar2=stc[:, 51:52],
                            op0=ALU.subtract, op1=ALU.mult))
                        ins = e.tensor_add(fbuf[tc][:, :], fbuf[tc][:, :],
                                           r2re[tc % 2][:, :])
                        out_dv[g5][tc] = I(ins, "dv", 1)

                    @blk.sync
                    def _(e, g5=g5, tc=tc, t=t):
                        W(e, "dv", out_dv[g5][tc])
                        out_sq[g5][tc] = dma(e, "ob%d" % tc, outr[t],
                                             fbuf[tc][:, :])

            @blk.sync
            def _(e):
                for tc in range(4):
                    W(e, "ob%d" % tc, n["ob%d" % tc])

    return nc


def kernel(L0_attn_x, L0_attn_Wq, L0_attn_Wk, L0_attn_Wv, L0_attn_Wg,
           L0_attn_Wo, L0_ffn_W1, L0_ffn_W2, L0_ffn_W3):
    x = np.ascontiguousarray(
        np.asarray(L0_attn_x, dtype=np.float32)).reshape(16384, D)
    ws = {k: np.ascontiguousarray(np.asarray(v, dtype=np.float32)) for k, v in
          (("wq", L0_attn_Wq), ("wk", L0_attn_Wk), ("wv", L0_attn_Wv),
           ("wg", L0_attn_Wg), ("wo", L0_attn_Wo), ("w1", L0_ffn_W1),
           ("w2", L0_ffn_W2), ("w3", L0_ffn_W3))}
    nc = _build()
    in_maps = []
    for c in range(NCORE):
        m = {"x": x[c * T:(c + 1) * T]}
        m.update(ws)
        m.update(_host_consts(c))
        in_maps.append(m)
    res = run_bass_kernel_spmd(nc, in_maps, core_ids=list(range(NCORE)))
    out = np.concatenate([res.results[c]["out"] for c in range(NCORE)], axis=0)
    return out.reshape(1, 16384, D)
